# version 10
# baseline (speedup 1.0000x reference)
import numpy as np
import concourse.bass as bass
import concourse.mybir as mybir
from concourse.bass_utils import run_bass_kernel_spmd

F32 = mybir.dt.float32
BF16 = mybir.dt.bfloat16
AF = mybir.ActivationFunctionType
OP = mybir.AluOpType
AX = mybir.AxisListType

B, L, D = 4, 4096, 2048
R, DH, I = 2048, 512, 8192
HL = 2048
T = 512
NS = HL // T
NDC = D // 128
NRC = R // 128
NH4 = DH // 128
NIC = I // 128
NHC = D // 128
TS = T + 2
EPS = 1e-5

DEBUG = False
DUMP = []

ENGINES = ("pe", "dve", "act", "pool", "sp")


class K:
    def __init__(self, nc):
        self.nc = nc
        self.cmds = {e: [] for e in ENGINES}
        self.sems = {}
        self.cnt = {}
        self.watermark = {e: {} for e in ENGINES}
        self.res = {}
        for e in ENGINES:
            self.sem(e)
        self.streams = {}
        for nm, depth in (("s_w", 10), ("s_x", 4), ("s_cast", 6), ("s_echo", 4),
                          ("s_hs", 3), ("s_hsr", 4), ("s_out", 3), ("s_dbg", 2)):
            self.streams[nm] = {"depth": depth, "n": 0,
                                "sems": [f"{nm}{j}" for j in range(depth)]}
            for j in range(depth):
                self.sem(f"{nm}{j}")

    def sem(self, name):
        if name not in self.sems:
            self.sems[name] = self.nc.alloc_semaphore(name)
            self.cnt[name] = 0
        return self.sems[name]

    def emit(self, eng, fn, waits=(), inc=None):
        w = []
        for mk in waits:
            if mk is None:
                continue
            sname, val = mk
            if val <= 0 or sname == eng:
                continue
            if self.watermark[eng].get(sname, 0) >= val:
                continue
            self.watermark[eng][sname] = val
            w.append((sname, val))
        self.cmds[eng].append((w, fn, inc))
        if inc is not None:
            self.cnt[inc[0]] += inc[1]
            return (inc[0], self.cnt[inc[0]])
        return None

    def dma(self, stream, eng, fn, waits=()):
        st = self.streams[stream]
        i = st["n"]
        st["n"] = i + 1
        j = i % st["depth"]
        rounds = i // st["depth"]
        sname = st["sems"][j]
        w = list(waits) + ([(sname, 16 * rounds)] if rounds else [])
        return self.emit(eng, fn, waits=w, inc=(sname, 16))

    def stream_tail(self, stream):
        st = self.streams[stream]
        return [(s, self.cnt[s]) for s in st["sems"] if self.cnt[s]]

    def mark(self, sem):
        return (sem, self.cnt[sem])

    def free(self, r):
        return list(self.res.get(r, []))

    def setfree(self, r, *marks):
        self.res[r] = [m for m in marks if m is not None]

    def addfree(self, r, *marks):
        self.res[r] = self.res.get(r, []) + [m for m in marks if m is not None]

    def finalize(self):
        with self.nc.Block() as blk:
            def body_for(ename):
                cmds, sems = self.cmds[ename], self.sems
                selfchain = ename in ("dve", "act", "pool")

                def body(e):
                    done = 0
                    for w, fn, inc in cmds:
                        if selfchain and done > 0:
                            e.wait_ge(sems[ename], done)
                        for sname, val in w:
                            e.wait_ge(sems[sname], val)
                        inst = fn(e)
                        if inc is not None:
                            inst.then_inc(sems[inc[0]], inc[1])
                            if inc[0] == ename:
                                done += inc[1]
                return body
            blk.sync(body_for("sp"))
            blk.gpsimd(body_for("pool"))
            blk.tensor(body_for("pe"))
            blk.vector(body_for("dve"))
            blk.scalar(body_for("act"))


def _r4(a):
    return np.ascontiguousarray(a, dtype=np.float32)


def stage_inputs(inp):
    x = np.asarray(inp["L0_attn_x"], dtype=np.float32)
    Wu = np.asarray(inp["L0_attn_Wu"], dtype=np.float32)
    bu = np.asarray(inp["L0_attn_bu"], dtype=np.float32)
    Wv = np.asarray(inp["L0_attn_Wv"], dtype=np.float32)
    bv = np.asarray(inp["L0_attn_bv"], dtype=np.float32)
    Wg1 = np.asarray(inp["L0_attn_Wg1"], dtype=np.float32)
    bg1 = np.asarray(inp["L0_attn_bg1"], dtype=np.float32)
    Wg2 = np.asarray(inp["L0_attn_Wg2"], dtype=np.float32)
    bg2 = np.asarray(inp["L0_attn_bg2"], dtype=np.float32)
    cw = np.asarray(inp["L0_attn_conv_w"], dtype=np.float32)
    cb = np.asarray(inp["L0_attn_conv_b"], dtype=np.float32)
    wg = np.asarray(inp["L0_ffn_w_gate"], dtype=np.float32)
    wup = np.asarray(inp["L0_ffn_w_up"], dtype=np.float32)
    wd = np.asarray(inp["L0_ffn_w_down"], dtype=np.float32)

    n = np.arange(I)
    old = 4 * (n % (I // 4)) + n // (I // 4)
    wg_p, wup_p, wd_p = wg[:, old], wup[:, old], wd[old, :]

    A_u = _r4(Wu.reshape(NRC, 128, NDC, 128).transpose(0, 3, 2, 1).reshape(NRC, 128, NDC * 128))
    A_v = _r4(Wv.reshape(NRC, 128, NDC, 128).transpose(0, 3, 2, 1).reshape(NRC, 128, NDC * 128))
    A_uv = _r4(Wu.reshape(NDC, 128, NRC, 128).transpose(2, 3, 0, 1).reshape(NRC, 128, NDC * 128))
    A_g1 = _r4(Wg1.reshape(NH4, 128, NDC, 128).transpose(0, 3, 2, 1).reshape(NH4, 128, NDC * 128))
    A_g2 = _r4(Wg2.reshape(NRC, 128, NH4, 128).transpose(0, 3, 2, 1).reshape(NRC, 128, NH4 * 128))
    A_gate = _r4(wg_p.reshape(NHC, 128, NIC, 128).transpose(2, 1, 0, 3).reshape(NIC, 128, NHC * 128))
    A_up = _r4(wup_p.reshape(NHC, 128, NIC, 128).transpose(2, 1, 0, 3).reshape(NIC, 128, NHC * 128))
    A_dn = _r4(wd_p.reshape(NIC, 128, 2, 8, 128).transpose(2, 0, 1, 3, 4).reshape(2 * NIC, 128, 8 * 128))

    def col16(vec):
        return _r4(vec.reshape(-1, 128).T)

    shared = dict(
        A_u=A_u, A_v=A_v, A_uv=A_uv, A_g1=A_g1, A_g2=A_g2,
        A_gate=A_gate, A_up=A_up, A_dn=A_dn,
        bu_c=col16(bu), bv_c=col16(bv), bg2_c=col16(bg2), bg1_c=col16(bg1),
        cb_c=col16(cb), w0_c=col16(cw[:, 0, 0]), w1_c=col16(cw[:, 0, 1]),
        w2_c=col16(cw[:, 0, 2]),
    )

    in_maps = []
    for c in range(8):
        b, h = c // 2, c % 2
        xf = np.zeros((D, HL + 2), dtype=np.float32)
        xf[:, 1:HL + 1] = x[b, h * HL:(h + 1) * HL, :].T
        if h == 1:
            xf[:, 0] = x[b, HL - 1, :]
        else:
            xf[:, HL + 1] = x[b, HL, :]
        halo = np.zeros((128, 80), dtype=np.float32)
        for j, col in enumerate([0, 1 + T, 1 + 2 * T, 1 + 3 * T, HL + 1]):
            halo[:, j * 16:(j + 1) * 16] = xf[:, col].reshape(16, 128).T
        m = dict(shared)
        m.update(xf=_r4(xf), xpf=_r4(x[b, 0:HL, :].T), xhalo=halo,
                 mask_c=np.full((128, 16), float(h), dtype=np.float32))
        in_maps.append(m)
    return in_maps


def build():
    nc = bass.Bass("TRN2", target_bir_lowering=False)
    k = K(nc)

    d = {}
    d["xf"] = nc.dram_tensor("xf", [D, HL + 2], F32, kind="ExternalInput")
    d["xpf"] = nc.dram_tensor("xpf", [D, HL], F32, kind="ExternalInput")
    d["xhalo"] = nc.dram_tensor("xhalo", [128, 80], F32, kind="ExternalInput")
    wshapes = {
        "A_u": [NRC, 128, NDC * 128], "A_v": [NRC, 128, NDC * 128],
        "A_uv": [NRC, 128, NDC * 128], "A_g1": [NH4, 128, NDC * 128],
        "A_g2": [NRC, 128, NH4 * 128], "A_gate": [NIC, 128, NHC * 128],
        "A_up": [NIC, 128, NHC * 128], "A_dn": [2 * NIC, 128, 8 * 128],
    }
    for nm, shp in wshapes.items():
        d[nm] = nc.dram_tensor(nm, shp, F32, kind="ExternalInput")
        d["s" + nm] = nc.dram_tensor("s" + nm, shp, BF16, kind="Internal")
    for nm in ["bu_c", "bv_c", "bg2_c", "cb_c", "w0_c", "w1_c", "w2_c", "mask_c"]:
        d[nm] = nc.dram_tensor(nm, [128, 16], F32, kind="ExternalInput")
    d["bg1_c"] = nc.dram_tensor("bg1_c", [128, 4], F32, kind="ExternalInput")
    d["out"] = nc.dram_tensor("out", [D, HL], F32, kind="ExternalOutput")
    d["hs_scr"] = nc.dram_tensor("hs_scr", [16, 128, 4 * T], BF16, kind="Internal")

    dbg = {}
    if DEBUG:
        for tag, shp, dt in [("xn0", [128, NDC * TS], BF16), ("g0", [128, NRC * T], BF16),
                             ("go0", [128, NRC * T], BF16),
                             ("Y0", [128, NDC * T], F32), ("yn0", [128, NDC * T], BF16),
                             ("carry0", [128, 16], F32), ("s0col", [128, 16], F32),
                             ("xnp0", [128, NDC * T], BF16)]:
            dbg[tag] = nc.dram_tensor("dbg_" + tag, shp, dt, kind="ExternalOutput")

    def S(name, shape, dt=F32):
        return nc.alloc_sbuf_tensor(name, shape, dt)

    xn = [S("xn0", [128, NDC * TS], BF16), S("xn1", [128, NDC * TS], BF16)]
    xbf = S("xbf", [128, 3 * T], BF16)
    xf32 = S("xf32", [128, 3 * T])
    xr = S("xr", [128, 2 * T])
    sqs = S("sqs", [128, 3 * T], BF16)
    g_sl = S("g_sl", [128, NRC * T], BF16)
    vgs = S("vgs", [128, T], BF16)
    goyn = S("goyn", [128, NRC * T], BF16)
    hfm = S("hfm", [128, NH4 * T], BF16)
    ug = S("ug", [128, T])
    kc = S("kc", [128, T])
    carry = S("carry", [128, 16])
    s0col = S("s0col", [128, 16])
    red1 = S("red1", [128, 1])
    Y = S("Y", [128, NDC * T])
    tmp1 = S("tmp1", [128, T])
    tcv = S("tcv", [128, T])
    hp = S("hp", [128, 4 * T], BF16)
    apl = S("apl", [128, 4 * T], BF16)
    tre = S("tre", [128, 4 * T], BF16)
    ssc = S("ssc", [128, 3 * T], BF16)
    hin = S("hin", [128, 3 * T], BF16)
    ost = S("ost", [128, 2 * T])
    mrow = S("mrow", [1, T])
    qrow = S("qrow", [1, T])
    sdrow = S("sdrow", [1, T])
    rrow = S("rrow", [1, T])
    nmrow = S("nmrow", [1, T])
    onesb = S("onesb", [128, 1], BF16)
    onesr = S("onesr", [1, 128])
    zero5 = S("zero5", [128, T], BF16)
    xnh = S("xnh", [128, 80], BF16)
    xhal = S("xhal", [128, 80])
    hrow = S("hrow", [1, 64])
    hb = S("hb", [128, 10])
    epsc = S("epsc", [1, 1])
    cols = {}
    for nm in ["bu_c", "bv_c", "bg2_c", "cb_c", "w0_c", "w1_c", "w2_c", "mask_c"]:
        cols[nm] = S(nm + "_s", [128, 16])
    cols["bg1_c"] = S("bg1_c_s", [128, 4])

    wb_g2 = S("wb_g2", [128, 2 * NH4 * 128], BF16)
    wb_uuv = S("wb_uuv", [128, 3 * NDC * 128], BF16)
    wb_vgu = S("wb_vgu", [128, 6 * NHC * 128], BF16)
    wb_dn = S("wb_dn", [128, 4 * 8 * 128], BF16)

    P = [nc.alloc_psum_tensor(f"P{i}", [128, T], F32) for i in range(8)]

    def bankwait(j):
        return k.free(("bank", j))

    def bankread(j, mark):
        k.setfree(("bank", j), mark)

    castmark = {}

    def bulk_cast(mat, s0, nsl):
        m = k.dma("s_cast", "pool", lambda e, a=mat, b=s0, n_=nsl: e.dma_start(
            d["sA_" + a][b:b + n_, :, :], d["A_" + a][b:b + n_, :, :]))
        for sl_ in range(s0, s0 + nsl):
            castmark[(mat, sl_)] = m

    class WSBuf:
        def __init__(self, buf, slab_free, nslots):
            self.buf, self.w = buf, slab_free
            self.lastuse = [[] for _ in range(nslots)]

        def view(self, slot):
            return self.buf[:, slot * self.w:(slot + 1) * self.w]

        def load(self, mat, slab, slot):
            dst = self.view(slot)
            srcb = d["sA_" + mat][slab, :, :]
            waits = list(self.lastuse[slot]) + [castmark[(mat, slab)]]
            m = k.dma("s_w", "sp", lambda e, a=dst, b=srcb: e.dma_start(a, b),
                      waits=waits)
            self.lastuse[slot] = []
            return m

        def used(self, slot, mark):
            if mark is not None:
                self.lastuse[slot].append(mark)

    ws_g2 = WSBuf(wb_g2, NH4 * 128, 2)
    ws_uuv = WSBuf(wb_uuv, NDC * 128, 3)
    ws_g1 = ws_uuv
    ws_vgu = WSBuf(wb_vgu, NHC * 128, 6)
    ws_dn = WSBuf(wb_dn, 8 * 128, 4)

    def dump(tag, src_ap, waits, fence=()):
        if DEBUG and tag in DUMP and tag in dbg:
            mk = k.dma("s_dbg", "sp",
                       lambda e, s=src_ap, t=tag: e.dma_start(dbg[t][:, :], s),
                       waits=waits)
            for key in fence:
                k.addfree(key, mk)
            return mk
        return None

    def ln_rows_bcast(msum, msq):
        k.emit("dve", lambda e: e.tensor_scalar_mul(mrow[:, :], P[6][0:1, :], 1.0 / D),
               waits=[msum] + k.free("rows"), inc=("dve", 1))
        k.emit("dve", lambda e: e.tensor_scalar_mul(qrow[:, :], P[7][0:1, :], 1.0 / D),
               waits=[msq], inc=("dve", 1))
        k.emit("dve", lambda e: e.tensor_mul(sdrow[:, :], mrow[:, :], mrow[:, :]),
               inc=("dve", 1))
        m4 = k.emit("dve", lambda e: e.tensor_sub(qrow[:, :], qrow[:, :], sdrow[:, :]),
                    inc=("dve", 1))
        m5 = k.emit("act", lambda e: e.activation(sdrow[:, :], qrow[:, :], AF.Sqrt,
                                                  bias=epsc[0:1, 0:1]),
                    waits=[m4, m_eps], inc=("act", 1))
        k.emit("dve", lambda e: e.reciprocal(rrow[:, :], sdrow[:, :]),
               waits=[m5], inc=("dve", 1))
        m7 = k.emit("dve", lambda e: e.scalar_tensor_tensor(
            nmrow[:, :], mrow[:, :], -1.0, rrow[:, :], OP.mult, OP.mult),
            inc=("dve", 1))
        mb1 = k.emit("pe", lambda e: e.matmul(P[6][:, :], onesr[:, :], rrow[:, :],
                                              start=True, stop=True),
                     waits=[m7, m_onesr], inc=("pe", 1))
        mb2 = k.emit("pe", lambda e: e.matmul(P[7][:, :], onesr[:, :], nmrow[:, :],
                                              start=True, stop=True),
                     waits=[m7], inc=("pe", 1))
        k.setfree("rows", mb2)
        return mb1, mb2

    def ln_full(dst_xn, x_dram, col0, W, off, dst_war):
        msum = msq = None
        ldm = []
        for j in range(3):
            ldm.append(k.dma("s_x", "act", lambda e, dc_=j, s_=j: e.dma_start(
                xf32[:, s_ * T:(s_ + 1) * T],
                x_dram[dc_ * 128:(dc_ + 1) * 128, col0:col0 + T]),
                waits=k.free(("xf32", j))))
        for dc in range(NDC):
            sl = dc % 3
            mcb = k.emit("act", lambda e, s_=sl: e.copy(
                xbf[:, s_ * T:(s_ + 1) * T], xf32[:, s_ * T:(s_ + 1) * T]),
                waits=[ldm[dc]] + k.free(("xbf", sl)), inc=("act", 1))
            k.setfree(("xf32", sl), mcb)
            if dc + 3 < NDC:
                ldm.append(k.dma("s_x", "act", lambda e, dc_=dc + 3, s_=sl: e.dma_start(
                    xf32[:, s_ * T:(s_ + 1) * T],
                    x_dram[dc_ * 128:(dc_ + 1) * 128, col0:col0 + T])))
            msum = k.emit("pe", lambda e, dc_=dc, s_=sl: e.matmul(
                P[6][0:1, :], onesb[:, :], xbf[:, s_ * T:(s_ + 1) * T],
                start=(dc_ == 0), stop=(dc_ == NDC - 1)),
                waits=[mcb, m_ones] + (bankwait(6) if dc == 0 else []),
                inc=("pe", 1))
            ma = k.emit("dve", lambda e, s_=sl: e.tensor_mul(
                sqs[:, s_ * T:(s_ + 1) * T], xbf[:, s_ * T:(s_ + 1) * T],
                xbf[:, s_ * T:(s_ + 1) * T]),
                waits=[mcb] + k.free(("sqs", sl)), inc=("dve", 1))
            msq = k.emit("pe", lambda e, dc_=dc, s_=sl: e.matmul(
                P[7][0:1, :], onesb[:, :], sqs[:, s_ * T:(s_ + 1) * T],
                start=(dc_ == 0), stop=(dc_ == NDC - 1)),
                waits=[ma] + (bankwait(7) if dc == 0 else []),
                inc=("pe", 1))
            k.setfree(("sqs", sl), msq)
            k.setfree(("xbf", sl), msq)
        mb1, mb2 = ln_rows_bcast(msum, msq)
        segm = []
        for dc in range(NDC):
            sl = dc % 2
            mx = k.dma("s_x", "act", lambda e, dc_=dc, s_=sl: e.dma_start(
                xr[:, s_ * T:(s_ + 1) * T],
                x_dram[dc_ * 128:(dc_ + 1) * 128, col0:col0 + T]),
                waits=k.free(("xr", sl)))
            mt = k.emit("dve", lambda e, s_=sl: e.tensor_mul(
                tmp1[:, :], xr[:, s_ * T:(s_ + 1) * T], P[6][:, :]),
                waits=[mx, mb1] + k.free("tmp1"), inc=("dve", 1))
            k.setfree(("xr", sl), mt)
            mn = k.emit("dve", lambda e, dc_=dc: e.tensor_add(
                dst_xn[:, dc_ * W + off:dc_ * W + off + T], tmp1[:, :], P[7][:, :]),
                waits=[mb2] + (list(dst_war) if dc == 0 else []), inc=("dve", 1))
            k.setfree("tmp1", mn)
            segm.append(mn)
        bankread(6, segm[-1])
        bankread(7, segm[-1])
        return segm

    bulk_cast("g1", 0, NH4)
    for p0 in range(0, NRC, 8):
        bulk_cast("g2", p0, 8)
    for p0 in range(0, NRC, 4):
        bulk_cast("u", p0, 4)

    mc = {}
    for nm in cols:
        mc[nm] = k.dma("s_w", "sp",
                       lambda e, n_=nm: e.dma_start(cols[n_][:, :], d[n_][:, :]))
    m_halo = k.dma("s_w", "sp", lambda e: e.dma_start(xhal[:, :], d["xhalo"][:, :]))
    m_ones = k.emit("dve", lambda e: e.memset(onesb[:, :], 1.0), inc=("dve", 1))
    m_onesr = k.emit("dve", lambda e: e.memset(onesr[:, :], 1.0), inc=("dve", 1))
    m_zero = k.emit("dve", lambda e: e.memset(zero5[:, :], 0.0), inc=("dve", 1))
    m_eps = k.emit("dve", lambda e: e.memset(epsc[:, :], EPS), inc=("dve", 1))

    def gphase(xn_ap, W, off, segm):
        hm = []
        for hc in range(NH4):
            b = hc % 4
            mw = ws_g1.load("g1", hc, hc % 2)
            pm = None
            for dc in range(NDC):
                pm = k.emit("pe", lambda e, hc_=hc, dc_=dc, b_=b, sl=hc % 2,
                            x_=xn_ap: e.matmul(
                    P[b_][:, :], ws_g1.view(sl)[:, dc_ * 128:(dc_ + 1) * 128],
                    x_[:, dc_ * W + off:dc_ * W + off + T],
                    start=(dc_ == 0), stop=(dc_ == NDC - 1)),
                    waits=[mw, segm[dc]] + (bankwait(b) if dc == 0 else []),
                    inc=("pe", 1) if dc == NDC - 1 else None)
            ws_g1.used(hc % 2, pm)
            me = k.emit("act", lambda e, hc_=hc, b_=b: e.activation(
                hfm[:, hc_ * T:(hc_ + 1) * T], P[b_][:, :], AF.Relu,
                bias=cols["bg1_c"][:, hc_:hc_ + 1]),
                waits=[pm, mc["bg1_c"]] + k.free(("hfm", hc)), inc=("act", 1))
            k.setfree(("hfm", hc))
            bankread(b, me)
            hm.append(me)
        gm = []
        for rc in range(NRC):
            b = rc % 6
            mw = ws_g2.load("g2", rc, rc % 2)
            pm = None
            for hc in range(NH4):
                pm = k.emit("pe", lambda e, rc_=rc, hc_=hc, b_=b, sl=rc % 2: e.matmul(
                    P[b_][:, :], ws_g2.view(sl)[:, hc_ * 128:(hc_ + 1) * 128],
                    hfm[:, hc_ * T:(hc_ + 1) * T],
                    start=(hc_ == 0), stop=(hc_ == NH4 - 1)),
                    waits=[mw, hm[hc]] + (bankwait(b) if hc == 0 else []),
                    inc=("pe", 1) if hc == NH4 - 1 else None)
            ws_g2.used(rc % 2, pm)
            for hc in range(NH4):
                k.addfree(("hfm", hc), pm)
            me = k.emit("act", lambda e, rc_=rc, b_=b: e.activation(
                g_sl[:, rc_ * T:(rc_ + 1) * T], P[b_][:, :], AF.Sigmoid,
                bias=cols["bg2_c"][:, rc_:rc_ + 1]),
                waits=[pm, mc["bg2_c"]] + k.free(("g", rc)), inc=("act", 1))
            bankread(b, me)
            k.setfree(("g", rc))
            gm.append(me)
        return gm

    xn_war = {0: [], 1: []}
    m_s0z = k.emit("dve", lambda e: e.memset(s0col[:, :], 0.0), inc=("dve", 1))
    segp_next = ln_full(xn[0], d["xpf"], 0, T, 0, [])
    for ss in range(NS):
        par_p = ss % 2
        xnp = xn[par_p]
        segp = segp_next
        mkd_xnp = dump("xnp0", xnp[:, 0:NDC * T], [segp[-1]]) if ss == 0 else None
        gm = gphase(xnp, T, 0, segp)
        if ss + 1 < NS:
            segp_next = ln_full(xn[1 - par_p], d["xpf"], (ss + 1) * T, T, 0,
                                xn_war[1 - par_p])
        war_pe = []
        for rc in range(NRC):
            b = rc % 6
            slot = rc % 3
            mw = ws_uuv.load("u", rc, slot)
            pm = None
            for dc in range(NDC):
                pm = k.emit("pe", lambda e, rc_=rc, dc_=dc, sl=slot, b_=b,
                            x_=xnp: e.matmul(
                    P[b_][:, :], ws_uuv.view(sl)[:, dc_ * 128:(dc_ + 1) * 128],
                    x_[:, dc_ * T:(dc_ + 1) * T],
                    start=(dc_ == 0), stop=(dc_ == NDC - 1)),
                    waits=[mw, segp[dc], gm[rc]] + (bankwait(b) if dc == 0 else []),
                    inc=("pe", 1) if dc == NDC - 1 else None)
            ws_uuv.used(slot, pm)
            mu = k.emit("dve", lambda e, rc_=rc, b_=b: e.scalar_tensor_tensor(
                ug[:, :], P[b_][:, :], cols["bu_c"][:, rc_:rc_ + 1],
                g_sl[:, rc_ * T:(rc_ + 1) * T], OP.add, OP.mult),
                waits=[pm, mc["bu_c"], m_s0z], inc=("dve", 1))
            bankread(b, mu)
            k.addfree(("g", rc), mu)
            k.emit("dve", lambda e: e.reduce_sum(red1[:, :], ug[:, :], axis=AX.X),
                   inc=("dve", 1))
            k.emit("dve", lambda e, rc_=rc: e.tensor_add(
                s0col[:, rc_:rc_ + 1], s0col[:, rc_:rc_ + 1], red1[:, :]),
                inc=("dve", 1))
            war_pe.append(pm)
        xn_war[par_p] = war_pe + ([mkd_xnp] if mkd_xnp is not None else [])

    m_hbf = k.emit("act", lambda e: e.copy(xnh[:, :], xhal[:, :]), waits=[m_halo],
                   inc=("act", 1))
    mhs = k.emit("pe", lambda e: e.matmul(P[6][0:1, 0:80], onesb[:, :], xnh[:, :],
                                          start=True, stop=True),
                 waits=[m_hbf, m_ones], inc=("pe", 1))
    mhq0 = k.emit("act", lambda e: e.activation(sqs[:, 0:80], xnh[:, :], AF.Square),
                  waits=[m_hbf] + k.free(("sqs", 0)), inc=("act", 1))
    mhq = k.emit("pe", lambda e: e.matmul(P[7][0:1, 0:80], onesb[:, :], sqs[:, 0:80],
                                          start=True, stop=True),
                 waits=[mhq0], inc=("pe", 1))
    k.setfree(("sqs", 0), mhq)
    mh1 = k.emit("dve", lambda e: e.reduce_sum(
        hrow[0:1, 0:5], P[6][0:1, 0:80].rearrange("p (j c) -> p j c", j=5), axis=AX.X),
        waits=[mhs], inc=("dve", 1))
    mh2 = k.emit("dve", lambda e: e.reduce_sum(
        hrow[0:1, 5:10], P[7][0:1, 0:80].rearrange("p (j c) -> p j c", j=5), axis=AX.X),
        waits=[mhq], inc=("dve", 1))
    k.emit("dve", lambda e: e.tensor_scalar_mul(hrow[0:1, 10:20], hrow[0:1, 0:10], 1.0 / D),
           inc=("dve", 1))
    k.emit("dve", lambda e: e.tensor_mul(hrow[0:1, 20:25], hrow[0:1, 10:15],
                                         hrow[0:1, 10:15]), inc=("dve", 1))
    mh5 = k.emit("dve", lambda e: e.tensor_sub(hrow[0:1, 25:30], hrow[0:1, 15:20],
                                               hrow[0:1, 20:25]), inc=("dve", 1))
    mh6 = k.emit("act", lambda e: e.activation(hrow[0:1, 30:35], hrow[0:1, 25:30],
                                               AF.Sqrt, bias=epsc[0:1, 0:1]),
                 waits=[mh5, m_eps], inc=("act", 1))
    mh7 = k.emit("dve", lambda e: e.reciprocal(hrow[0:1, 35:40], hrow[0:1, 30:35]),
                 waits=[mh6], inc=("dve", 1))
    mh8 = k.emit("dve", lambda e: e.scalar_tensor_tensor(
        hrow[0:1, 40:45], hrow[0:1, 10:15], -1.0, hrow[0:1, 35:40], OP.mult, OP.mult),
        inc=("dve", 1))
    mh9 = k.emit("pe", lambda e: e.matmul(P[6][:, 0:10], onesr[:, :],
                                          hrow[0:1, 35:45], start=True, stop=True),
                 waits=[mh8, m_onesr], inc=("pe", 1))
    mh10 = k.emit("dve", lambda e: e.tensor_copy(hb[:, :], P[6][:, 0:10]),
                  waits=[mh9], inc=("dve", 1))
    bankread(6, mh10)
    bankread(7, mh2)
    m_xnh = None
    for j in range(5):
        m_xnh = k.emit("dve", lambda e, j_=j: e.tensor_scalar(
            xnh[:, j_ * 16:(j_ + 1) * 16], xhal[:, j_ * 16:(j_ + 1) * 16],
            hb[:, j_:j_ + 1], hb[:, 5 + j_:6 + j_], OP.mult, OP.add),
            waits=[mh10], inc=("dve", 1))

    seg0 = ln_full(xn[0], d["xf"], 1, TS, 1, xn_war[0])
    m_lh0 = k.emit("dve", lambda e: e.tensor_copy(
        xn[0][:, 0:NDC * TS:TS], xnh[:, 0:16]), waits=[m_xnh], inc=("dve", 1))
    m_rh0 = k.emit("dve", lambda e: e.tensor_copy(
        xn[0][:, TS - 1:NDC * TS:TS], xnh[:, 16:32]), inc=("dve", 1))
    xn_seg = {0: seg0}
    xn_halo = {0: m_rh0}


    dump("s0col", s0col[:, :], [k.emit("dve", lambda e: e.nop(), inc=("dve", 1))])
    m_carry = k.emit("dve", lambda e: e.tensor_mul(
        carry[:, :], s0col[:, :], cols["mask_c"][:, :]),
        waits=[mc["mask_c"]], inc=("dve", 1))
    dump("carry0", carry[:, :], [m_carry], fence=[("carryd",)])

    y_ready = [None] * NDC
    ln2_cover = None

    for i in range(NS):
        par = i % 2
        col0 = 1 + i * T
        xni = xn[par]
        segm = xn_seg[i]

        if i == 0:
            for p0 in range(0, NRC, 4):
                bulk_cast("v", p0, 4)
                bulk_cast("uv", p0, 4)

        gm = gphase(xni, TS, 1, segm)
        if i == 0:
            dump("g0", g_sl[:, :], [gm[-1]], fence=[("g", r_) for r_ in range(NRC)])

        if i == 0:
            for p0 in range(0, NIC, 4):
                bulk_cast("gate", p0, 4)
                bulk_cast("up", p0, 4)

        gom = []
        for rc in range(NRC):
            bv_ = (2 * rc) % 6
            bu_ = (2 * rc + 1) % 6
            mwv = ws_vgu.load("v", rc, rc % 3)
            pmv = None
            for dc in range(NDC):
                pmv = k.emit("pe", lambda e, rc_=rc, dc_=dc, sl=rc % 3, b_=bv_,
                             x_=xni: e.matmul(
                    P[b_][:, :], ws_vgu.view(sl)[:, dc_ * 128:(dc_ + 1) * 128],
                    x_[:, dc_ * TS + 1:dc_ * TS + 1 + T],
                    start=(dc_ == 0), stop=(dc_ == NDC - 1)),
                    waits=[mwv, segm[dc], gm[rc]] + (bankwait(bv_) if dc == 0 else []),
                    inc=("pe", 1) if dc == NDC - 1 else None)
            ws_vgu.used(rc % 3, pmv)
            mv = k.emit("dve", lambda e, rc_=rc, b_=bv_: e.scalar_tensor_tensor(
                vgs[:, :], P[b_][:, :], cols["bv_c"][:, rc_:rc_ + 1],
                g_sl[:, rc_ * T:(rc_ + 1) * T], OP.add, OP.mult),
                waits=[pmv, mc["bv_c"]], inc=("dve", 1))
            bankread(bv_, mv)
            mwu = ws_uuv.load("u", rc, rc % 3)
            pmu = None
            for dc in range(NDC):
                pmu = k.emit("pe", lambda e, rc_=rc, dc_=dc, sl=rc % 3, b_=bu_,
                             x_=xni: e.matmul(
                    P[b_][:, :], ws_uuv.view(sl)[:, dc_ * 128:(dc_ + 1) * 128],
                    x_[:, dc_ * TS + 1:dc_ * TS + 1 + T],
                    start=(dc_ == 0), stop=(dc_ == NDC - 1)),
                    waits=[mwu, segm[dc]] + (bankwait(bu_) if dc == 0 else []),
                    inc=("pe", 1) if dc == NDC - 1 else None)
            ws_uuv.used(rc % 3, pmu)
            mu = k.emit("dve", lambda e, rc_=rc, b_=bu_: e.scalar_tensor_tensor(
                ug[:, :], P[b_][:, :], cols["bu_c"][:, rc_:rc_ + 1],
                g_sl[:, rc_ * T:(rc_ + 1) * T], OP.add, OP.mult),
                waits=[pmu], inc=("dve", 1))
            bankread(bu_, mu)
            k.addfree(("g", rc), mu)
            k.emit("dve", lambda e, rc_=rc: e.tensor_tensor_scan(
                kc[:, :], ug[:, :], zero5[:, :], carry[:, rc_:rc_ + 1],
                OP.add, OP.add),
                waits=[m_zero, m_carry], inc=("dve", 1))
            k.emit("dve", lambda e, rc_=rc: e.tensor_copy(
                carry[:, rc_:rc_ + 1], kc[:, T - 1:T]),
                waits=k.free(("carryd",)), inc=("dve", 1))
            mg = k.emit("dve", lambda e, rc_=rc: e.tensor_mul(
                goyn[:, rc_ * T:(rc_ + 1) * T], kc[:, :], vgs[:, :]),
                waits=k.free(("gy", rc)), inc=("dve", 1))
            k.setfree(("gy", rc))
            gom.append(mg)
            dc = rc
            base = dc * TS
            c1 = k.emit("pool", lambda e, b_=base, dc_=dc, x_=xni: e.tensor_scalar(
                tcv[:, :], x_[:, b_:b_ + T], cols["w0_c"][:, dc_:dc_ + 1],
                cols["cb_c"][:, dc_:dc_ + 1], OP.mult, OP.add),
                waits=[segm[dc], xn_halo[i], mc["w0_c"], mc["cb_c"]] + k.free("tcv"),
                inc=("pool", 1))
            conv_c1_last = c1
            k.emit("dve", lambda e, b_=base, dc_=dc, x_=xni: e.scalar_tensor_tensor(
                tcv[:, :], x_[:, b_ + 1:b_ + 1 + T], cols["w1_c"][:, dc_:dc_ + 1],
                tcv[:, :], OP.mult, OP.add),
                waits=[mc["w1_c"], c1], inc=("dve", 1))
            c3 = k.emit("dve", lambda e, b_=base, dc_=dc, x_=xni: e.scalar_tensor_tensor(
                tcv[:, :], x_[:, b_ + 2:b_ + 2 + T], cols["w2_c"][:, dc_:dc_ + 1],
                tcv[:, :], OP.mult, OP.add),
                waits=[mc["w2_c"]], inc=("dve", 1))
            mx = k.dma("s_x", "act", lambda e, dc_=dc, s_=dc % 2, c0=col0: e.dma_start(
                xr[:, s_ * T:(s_ + 1) * T],
                d["xf"][dc_ * 128:(dc_ + 1) * 128, c0:c0 + T]),
                waits=k.free(("xr", dc % 2)))
            cy = k.emit("dve", lambda e, dc_=dc, s_=dc % 2: e.tensor_add(
                Y[:, dc_ * T:(dc_ + 1) * T], tcv[:, :], xr[:, s_ * T:(s_ + 1) * T]),
                waits=[c3, mx] + k.free(("Y", dc)), inc=("dve", 1))
            k.setfree(("xr", dc % 2), cy)
            k.setfree("tcv", cy)
            k.setfree(("Y", dc))
            y_ready[dc] = cy
        if i == 0:
            dump("go0", goyn[:, :], [gom[-1]], fence=[("gy", r_) for r_ in range(NRC)])

        grp = None
        for half in range(2):
            rc_marks = []
            for rc in range(NRC):
                slot = rc % 3
                mw = ws_uuv.load("uv", rc, slot)
                pm = None
                for j in range(8):
                    dc = half * 8 + j
                    w0 = [mw, gom[rc]]
                    if rc == 0:
                        w0 = w0 + bankwait(j)
                    pm = k.emit("pe", lambda e, rc_=rc, dc_=dc, j_=j, sl=slot: e.matmul(
                        P[j_][:, :], ws_uuv.view(sl)[:, dc_ * 128:(dc_ + 1) * 128],
                        goyn[:, rc_ * T:(rc_ + 1) * T],
                        start=(rc_ == 0), stop=(rc_ == NRC - 1)),
                        waits=w0, inc=("pe", 1) if j == 7 else None)
                ws_uuv.used(slot, pm)
                rc_marks.append(pm)
            grp = rc_marks[-1]
            for rc in range(NRC):
                k.addfree(("gy", rc), grp)
            xn_war[par] = [grp, conv_c1_last]
            for j in range(8):
                dc = half * 8 + j
                my = k.emit("dve", lambda e, dc_=dc, j_=j: e.tensor_add(
                    Y[:, dc_ * T:(dc_ + 1) * T], P[j_][:, :],
                    Y[:, dc_ * T:(dc_ + 1) * T]),
                    waits=[grp, y_ready[dc]], inc=("dve", 1))
                bankread(j, my)
                y_ready[dc] = my

        if i == 0:
            for p0 in range(0, 2 * NIC, 8):
                bulk_cast("dn", p0, 8)
        msum = None
        for dc in range(NDC):
            sl = dc % 2
            w0 = [y_ready[dc]] + k.free(("y2", sl))
            if dc < 2:
                w0 += k.free(("xbf", 0)) + k.free(("xbf", 1))
            mcb = k.emit("act", lambda e, dc_=dc, s_=sl: e.copy(
                xbf[:, s_ * T:(s_ + 1) * T], Y[:, dc_ * T:(dc_ + 1) * T]),
                waits=w0, inc=("act", 1))
            msum = k.emit("pe", lambda e, dc_=dc, s_=sl: e.matmul(
                P[6][0:1, :], onesb[:, :], xbf[:, s_ * T:(s_ + 1) * T],
                start=(dc_ == 0), stop=(dc_ == NDC - 1)),
                waits=[mcb] + (bankwait(6) if dc == 0 else []), inc=("pe", 1))
            k.setfree(("y2", sl), msum)
        msq = None
        for dc in range(NDC):
            sl = dc % 2
            mcb2 = k.emit("act", lambda e, dc_=dc, s_=sl: e.copy(
                xbf[:, s_ * T:(s_ + 1) * T], Y[:, dc_ * T:(dc_ + 1) * T]),
                waits=k.free(("y2", sl)), inc=("act", 1))
            ma = k.emit("act", lambda e, s_=sl: e.activation(
                sqs[:, s_ * T:(s_ + 1) * T], xbf[:, s_ * T:(s_ + 1) * T], AF.Square),
                waits=k.free(("sqs", sl)), inc=("act", 1))
            msq = k.emit("pe", lambda e, dc_=dc, s_=sl: e.matmul(
                P[7][0:1, :], onesb[:, :], sqs[:, s_ * T:(s_ + 1) * T],
                start=(dc_ == 0), stop=(dc_ == NDC - 1)),
                waits=[ma] + (bankwait(7) if dc == 0 else []), inc=("pe", 1))
            k.setfree(("sqs", sl), msq)
            k.setfree(("y2", sl), msq)
        k.setfree(("xbf", 0), msq)
        k.setfree(("xbf", 1), msq)
        ln2_cover = msq
        mb1, mb2 = ln_rows_bcast(msum, msq)
        yn_m = []
        for dc in range(NDC):
            mt = k.emit("dve", lambda e, dc_=dc: e.tensor_mul(
                tmp1[:, :], Y[:, dc_ * T:(dc_ + 1) * T], P[6][:, :]),
                waits=[mb1] + k.free("tmp1"), inc=("dve", 1))
            mn = k.emit("dve", lambda e, dc_=dc: e.tensor_add(
                goyn[:, dc_ * T:(dc_ + 1) * T], tmp1[:, :], P[7][:, :]),
                waits=[mb2] + k.free(("gy", dc)), inc=("dve", 1))
            k.setfree("tmp1", mn)
            k.setfree(("gy", dc))
            yn_m.append(mn)
        bankread(6, yn_m[-1])
        bankread(7, yn_m[-1])
        if i == 0:
            dump("yn0", goyn[:, :], [yn_m[-1]], fence=[("gy", d_) for d_ in range(NDC)])
        dump("Y0", Y[:, :], [y_ready[NDC - 1]],
             fence=[("Y", d_) for d_ in range(NDC)]) if i == 0 else None

        hs_w = []
        for gc in range(NRC):
            abs_m = []
            for pl in range(4):
                ic = pl * 16 + gc
                sg = (gc * 4 + pl) % 3
                slg = (gc * 4 + pl) % 3
                mwg = ws_vgu.load("gate", ic, slg)
                mwu = ws_vgu.load("up", ic, 3 + slg)
                pg = None
                for hc in range(NHC):
                    pg = k.emit("pe", lambda e, hc_=hc, s_=sg, sl=slg: e.matmul(
                        P[s_][:, :], ws_vgu.view(sl)[:, hc_ * 128:(hc_ + 1) * 128],
                        goyn[:, hc_ * T:(hc_ + 1) * T],
                        start=(hc_ == 0), stop=(hc_ == NHC - 1)),
                        waits=[mwg, yn_m[hc]] + (bankwait(sg) if hc == 0 else []),
                        inc=("pe", 1) if hc == NHC - 1 else None)
                pu = None
                for hc in range(NHC):
                    pu = k.emit("pe", lambda e, hc_=hc, s_=sg, sl=3 + slg: e.matmul(
                        P[3 + s_][:, :], ws_vgu.view(sl)[:, hc_ * 128:(hc_ + 1) * 128],
                        goyn[:, hc_ * T:(hc_ + 1) * T],
                        start=(hc_ == 0), stop=(hc_ == NHC - 1)),
                        waits=[mwu, yn_m[hc]] + (bankwait(3 + sg) if hc == 0 else []),
                        inc=("pe", 1) if hc == NHC - 1 else None)
                ws_vgu.used(slg, pg)
                ws_vgu.used(3 + slg, pu)
                for dc in range(NDC):
                    k.addfree(("gy", dc), pu)
                msl = k.emit("act", lambda e, s_=sg: e.activation(
                    ssc[:, s_ * T:(s_ + 1) * T], P[s_][:, :], AF.Silu),
                    waits=[pg] + k.free(("ssc", sg)), inc=("act", 1))
                bankread(sg, msl)
                mh = k.emit("dve", lambda e, pl_=pl, s_=sg: e.tensor_mul(
                    hp[:, pl_ * T:(pl_ + 1) * T], P[3 + s_][:, :],
                    ssc[:, s_ * T:(s_ + 1) * T]),
                    waits=[pu, msl] + k.free(("hp", pl)), inc=("dve", 1))
                bankread(3 + sg, mh)
                k.setfree(("ssc", sg), mh)
                k.setfree(("hp", pl))
                ma = k.emit("act", lambda e, pl_=pl: e.activation(
                    apl[:, pl_ * T:(pl_ + 1) * T], hp[:, pl_ * T:(pl_ + 1) * T],
                    AF.Abs),
                    waits=[mh] + k.free(("apl", pl)), inc=("act", 1))
                k.setfree(("apl", pl))
                abs_m.append(ma)
            t1 = k.emit("dve", lambda e: e.tensor_tensor(
                tre[:, 0:T], apl[:, 0:T], apl[:, T:2 * T], OP.min),
                waits=abs_m[0:2] + k.free("tre"), inc=("dve", 1))
            k.emit("dve", lambda e: e.tensor_tensor(
                tre[:, T:2 * T], apl[:, 0:T], apl[:, T:2 * T], OP.max), inc=("dve", 1))
            k.emit("dve", lambda e: e.tensor_tensor(
                tre[:, 2 * T:3 * T], apl[:, 2 * T:3 * T], apl[:, 3 * T:4 * T], OP.min),
                waits=abs_m[2:4], inc=("dve", 1))
            k.emit("dve", lambda e: e.tensor_tensor(
                tre[:, 3 * T:4 * T], apl[:, 2 * T:3 * T], apl[:, 3 * T:4 * T], OP.max),
                inc=("dve", 1))
            k.emit("dve", lambda e: e.tensor_tensor(
                tre[:, 0:T], tre[:, 0:T], tre[:, 2 * T:3 * T], OP.max), inc=("dve", 1))
            k.emit("dve", lambda e: e.tensor_tensor(
                tre[:, T:2 * T], tre[:, T:2 * T], tre[:, 3 * T:4 * T], OP.min),
                inc=("dve", 1))
            t7 = k.emit("dve", lambda e: e.tensor_tensor(
                tre[:, 2 * T:3 * T], tre[:, 0:T], tre[:, T:2 * T], OP.min),
                inc=("dve", 1))
            mm = []
            for pl in range(4):
                k.emit("dve", lambda e, pl_=pl: e.tensor_tensor(
                    tre[:, 3 * T:4 * T], apl[:, pl_ * T:(pl_ + 1) * T],
                    tre[:, 2 * T:3 * T], OP.is_gt),
                    waits=[t7], inc=("dve", 1))
                mk2 = k.emit("dve", lambda e, pl_=pl: e.tensor_mul(
                    apl[:, pl_ * T:(pl_ + 1) * T], hp[:, pl_ * T:(pl_ + 1) * T],
                    tre[:, 3 * T:4 * T]),
                    inc=("dve", 1))
                k.addfree(("hp", pl), mk2)
                mm.append(mk2)
            k.addfree("tre", mm[-1])
            mwr = k.dma("s_hs", "pool", lambda e, gc_=gc: e.dma_start(
                d["hs_scr"][gc_, :, :], apl[:, :]),
                waits=[mm[-1]] + k.free(("hsd", gc)))
            k.setfree(("hsd", gc))
            for pl in range(4):
                k.addfree(("apl", pl), mwr)
            hs_w.append(mwr)
            if gc == 0 and i + 1 < NS:
                segn = ln_full(xn[1 - par], d["xf"], 1 + (i + 1) * T, TS, 1,
                               xn_war[1 - par])
                mhl = k.emit("dve", lambda e, p_=par: e.tensor_copy(
                    xn[1 - p_][:, 0:NDC * TS:TS], xn[p_][:, T:NDC * TS:TS]),
                    inc=("dve", 1))
                mhr = k.emit("dve", lambda e, p_=par, i_=i: e.tensor_copy(
                    xn[1 - p_][:, TS - 1:NDC * TS:TS],
                    xnh[:, (i_ + 2) * 16:(i_ + 3) * 16]),
                    waits=[m_xnh], inc=("dve", 1))
                xn_seg[i + 1] = segn
                xn_halo[i + 1] = mhr

        for half in range(2):
            first_w = [bankwait(j) for j in range(8)]
            pm = None
            for jj in range(NIC):
                gc, pl = jj // 4, jj % 4
                ic = pl * 16 + gc
                sl = jj % 3
                sw = jj % 4
                md = ws_dn.load("dn", half * NIC + ic, sw)
                mi = k.dma("s_hsr", "act", lambda e, gc_=gc, pl_=pl, s_=sl: e.dma_start(
                    hin[:, s_ * T:(s_ + 1) * T],
                    d["hs_scr"][gc_, :, pl_ * T:(pl_ + 1) * T]),
                    waits=[hs_w[gc]] + k.free(("hin", sl)))
                k.addfree(("hsd", gc), mi)
                for hc8 in range(8):
                    w0 = [md, mi]
                    if jj == 0:
                        w0 = w0 + first_w[hc8]
                    pm = k.emit("pe", lambda e, h8=hc8, s_=sl, s2=sw, jj_=jj: e.matmul(
                        P[h8][:, :], ws_dn.view(s2)[:, h8 * 128:(h8 + 1) * 128],
                        hin[:, s_ * T:(s_ + 1) * T],
                        start=(jj_ == 0), stop=(jj_ == NIC - 1)),
                        waits=w0, inc=("pe", 1) if hc8 == 7 else None)
                ws_dn.used(sw, pm)
                k.setfree(("hin", sl), pm)
            grp_dn = pm
            for hc8 in range(8):
                hc = half * 8 + hc8
                sl = hc % 2
                mo = k.emit("dve", lambda e, hc_=hc, h8=hc8, s_=sl: e.tensor_add(
                    ost[:, s_ * T:(s_ + 1) * T], P[h8][:, :],
                    Y[:, hc_ * T:(hc_ + 1) * T]),
                    waits=[grp_dn] + k.free(("ost", sl)) + k.free(("Y", hc)),
                    inc=("dve", 1))
                bankread(hc8, mo)
                k.setfree(("Y", hc), mo, ln2_cover)
                mo2 = k.dma("s_out", "sp", lambda e, hc_=hc, s_=sl, i_=i: e.dma_start(
                    d["out"][hc_ * 128:(hc_ + 1) * 128, i_ * T:(i_ + 1) * T],
                    ost[:, s_ * T:(s_ + 1) * T]),
                    waits=[mo])
                k.setfree(("ost", sl), mo2)

    k.emit("pool", lambda e: e.nop(),
           waits=(k.stream_tail("s_out") + k.stream_tail("s_dbg")
                  + k.stream_tail("s_cast") + k.stream_tail("s_hs")),
           inc=("pool", 1))

    k.finalize()
    return nc


LAST_RESULTS = None


def kernel(**inputs):
    global LAST_RESULTS
    in_maps = stage_inputs(inputs)
    nc = build()
    res = run_bass_kernel_spmd(nc, in_maps, core_ids=list(range(8)))
    LAST_RESULTS = res
    out = np.empty((B, L, D), dtype=np.float32)
    for c in range(8):
        b, h = c // 2, c % 2
        out[b, h * HL:(h + 1) * HL, :] = res.results[c]["out"].T
    return out


if __name__ == "__main__":
    import reference as ref
    inp = ref.setup_inputs(0)
    got = kernel(**{k_: np.asarray(v) for k_, v in inp.items()})
    want = np.asarray(ref.reference(**inp))
    print("var ratio:", float(np.mean((got - want) ** 2) / np.mean(want ** 2)))
    print("max abs:", float(np.abs(got - want).max()))


# version 11
# speedup vs baseline: 1.1220x; 1.1220x over previous
import numpy as np
import concourse.bass as bass
import concourse.mybir as mybir
from concourse.bass_utils import run_bass_kernel_spmd

F32 = mybir.dt.float32
BF16 = mybir.dt.bfloat16
AF = mybir.ActivationFunctionType
OP = mybir.AluOpType
AX = mybir.AxisListType

B, L, D = 4, 4096, 2048
R, DH, I = 2048, 512, 8192
HL = 2048
T = 512
NS = HL // T
NDC = D // 128
NRC = R // 128
NH4 = DH // 128
NIC = I // 128
NHC = D // 128
TS = T + 2
EPS = 1e-5

DEBUG = False
DUMP = []

ENGINES = ("pe", "dve", "act", "pool", "sp")


class K:
    def __init__(self, nc):
        self.nc = nc
        self.cmds = {e: [] for e in ENGINES}
        self.sems = {}
        self.cnt = {}
        self.watermark = {e: {} for e in ENGINES}
        self.res = {}
        for e in ENGINES:
            self.sem(e)
        self.streams = {}
        for nm, depth in (("s_w", 16), ("s_x", 6), ("s_cast", 6), ("s_echo", 4),
                          ("s_hs", 3), ("s_hsr", 6), ("s_out", 3), ("s_dbg", 2)):
            self.streams[nm] = {"depth": depth, "n": 0,
                                "sems": [f"{nm}{j}" for j in range(depth)]}
            for j in range(depth):
                self.sem(f"{nm}{j}")

    def sem(self, name):
        if name not in self.sems:
            self.sems[name] = self.nc.alloc_semaphore(name)
            self.cnt[name] = 0
        return self.sems[name]

    def emit(self, eng, fn, waits=(), inc=None):
        w = []
        for mk in waits:
            if mk is None:
                continue
            sname, val = mk
            if val <= 0 or sname == eng:
                continue
            if self.watermark[eng].get(sname, 0) >= val:
                continue
            self.watermark[eng][sname] = val
            w.append((sname, val))
        self.cmds[eng].append((w, fn, inc))
        if inc is not None:
            self.cnt[inc[0]] += inc[1]
            return (inc[0], self.cnt[inc[0]])
        return None

    def dma(self, stream, eng, fn, waits=()):
        st = self.streams[stream]
        i = st["n"]
        st["n"] = i + 1
        j = i % st["depth"]
        rounds = i // st["depth"]
        sname = st["sems"][j]
        w = list(waits) + ([(sname, 16 * rounds)] if rounds else [])
        return self.emit(eng, fn, waits=w, inc=(sname, 16))

    def stream_tail(self, stream):
        st = self.streams[stream]
        return [(s, self.cnt[s]) for s in st["sems"] if self.cnt[s]]

    def mark(self, sem):
        return (sem, self.cnt[sem])

    def free(self, r):
        return list(self.res.get(r, []))

    def setfree(self, r, *marks):
        self.res[r] = [m for m in marks if m is not None]

    def addfree(self, r, *marks):
        self.res[r] = self.res.get(r, []) + [m for m in marks if m is not None]

    def finalize(self):
        with self.nc.Block() as blk:
            def body_for(ename):
                cmds, sems = self.cmds[ename], self.sems
                selfchain = ename in ("dve", "act", "pool")

                def body(e):
                    done = 0
                    for w, fn, inc in cmds:
                        if selfchain and done > 0:
                            e.wait_ge(sems[ename], done)
                        for sname, val in w:
                            e.wait_ge(sems[sname], val)
                        inst = fn(e)
                        if inc is not None:
                            inst.then_inc(sems[inc[0]], inc[1])
                            if inc[0] == ename:
                                done += inc[1]
                return body
            blk.sync(body_for("sp"))
            blk.gpsimd(body_for("pool"))
            blk.tensor(body_for("pe"))
            blk.vector(body_for("dve"))
            blk.scalar(body_for("act"))


def _r4(a):
    return np.ascontiguousarray(a, dtype=np.float32)


def stage_inputs(inp):
    x = np.asarray(inp["L0_attn_x"], dtype=np.float32)
    Wu = np.asarray(inp["L0_attn_Wu"], dtype=np.float32)
    bu = np.asarray(inp["L0_attn_bu"], dtype=np.float32)
    Wv = np.asarray(inp["L0_attn_Wv"], dtype=np.float32)
    bv = np.asarray(inp["L0_attn_bv"], dtype=np.float32)
    Wg1 = np.asarray(inp["L0_attn_Wg1"], dtype=np.float32)
    bg1 = np.asarray(inp["L0_attn_bg1"], dtype=np.float32)
    Wg2 = np.asarray(inp["L0_attn_Wg2"], dtype=np.float32)
    bg2 = np.asarray(inp["L0_attn_bg2"], dtype=np.float32)
    cw = np.asarray(inp["L0_attn_conv_w"], dtype=np.float32)
    cb = np.asarray(inp["L0_attn_conv_b"], dtype=np.float32)
    wg = np.asarray(inp["L0_ffn_w_gate"], dtype=np.float32)
    wup = np.asarray(inp["L0_ffn_w_up"], dtype=np.float32)
    wd = np.asarray(inp["L0_ffn_w_down"], dtype=np.float32)

    n = np.arange(I)
    old = 4 * (n % (I // 4)) + n // (I // 4)
    wg_p, wup_p, wd_p = wg[:, old], wup[:, old], wd[old, :]

    A_u = _r4(Wu.reshape(NRC, 128, NDC, 128).transpose(0, 3, 2, 1).reshape(NRC, 128, NDC * 128))
    A_v = _r4(Wv.reshape(NRC, 128, NDC, 128).transpose(0, 3, 2, 1).reshape(NRC, 128, NDC * 128))
    A_uv = _r4(Wu.reshape(NDC, 128, NRC, 128).transpose(2, 3, 0, 1).reshape(NRC, 128, NDC * 128))
    A_g1 = _r4(Wg1.reshape(NH4, 128, NDC, 128).transpose(0, 3, 2, 1).reshape(NH4, 128, NDC * 128))
    A_g2 = _r4(Wg2.reshape(NRC, 128, NH4, 128).transpose(0, 3, 2, 1).reshape(NRC, 128, NH4 * 128))
    A_gate = _r4(wg_p.reshape(NHC, 128, NIC, 128).transpose(2, 1, 0, 3).reshape(NIC, 128, NHC * 128))
    A_up = _r4(wup_p.reshape(NHC, 128, NIC, 128).transpose(2, 1, 0, 3).reshape(NIC, 128, NHC * 128))
    A_dn = _r4(wd_p.reshape(NIC, 128, 2, 8, 128).transpose(2, 0, 1, 3, 4).reshape(2 * NIC, 128, 8 * 128))

    def col16(vec):
        return _r4(vec.reshape(-1, 128).T)

    shared = dict(
        A_u=A_u, A_v=A_v, A_uv=A_uv, A_g1=A_g1, A_g2=A_g2,
        A_gate=A_gate, A_up=A_up, A_dn=A_dn,
        bu_c=col16(bu), bv_c=col16(bv), bg2_c=col16(bg2), bg1_c=col16(bg1),
        cb_c=col16(cb), w0_c=col16(cw[:, 0, 0]), w1_c=col16(cw[:, 0, 1]),
        w2_c=col16(cw[:, 0, 2]),
    )

    in_maps = []
    for c in range(8):
        b, h = c // 2, c % 2
        xf = np.zeros((D, HL + 2), dtype=np.float32)
        xf[:, 1:HL + 1] = x[b, h * HL:(h + 1) * HL, :].T
        if h == 1:
            xf[:, 0] = x[b, HL - 1, :]
        else:
            xf[:, HL + 1] = x[b, HL, :]
        halo = np.zeros((128, 80), dtype=np.float32)
        for j, col in enumerate([0, 1 + T, 1 + 2 * T, 1 + 3 * T, HL + 1]):
            halo[:, j * 16:(j + 1) * 16] = xf[:, col].reshape(16, 128).T
        m = dict(shared)
        m.update(xf=_r4(xf), xpf=_r4(x[b, 0:HL, :].T), xhalo=halo,
                 mask_c=np.full((128, 16), float(h), dtype=np.float32))
        in_maps.append(m)
    return in_maps


def build():
    nc = bass.Bass("TRN2", target_bir_lowering=False)
    k = K(nc)

    d = {}
    d["xf"] = nc.dram_tensor("xf", [D, HL + 2], F32, kind="ExternalInput")
    d["xpf"] = nc.dram_tensor("xpf", [D, HL], F32, kind="ExternalInput")
    d["xhalo"] = nc.dram_tensor("xhalo", [128, 80], F32, kind="ExternalInput")
    wshapes = {
        "A_u": [NRC, 128, NDC * 128], "A_v": [NRC, 128, NDC * 128],
        "A_uv": [NRC, 128, NDC * 128], "A_g1": [NH4, 128, NDC * 128],
        "A_g2": [NRC, 128, NH4 * 128], "A_gate": [NIC, 128, NHC * 128],
        "A_up": [NIC, 128, NHC * 128], "A_dn": [2 * NIC, 128, 8 * 128],
    }
    for nm, shp in wshapes.items():
        d[nm] = nc.dram_tensor(nm, shp, F32, kind="ExternalInput")
        d["s" + nm] = nc.dram_tensor("s" + nm, shp, BF16, kind="Internal")
    for nm in ["bu_c", "bv_c", "bg2_c", "cb_c", "w0_c", "w1_c", "w2_c", "mask_c"]:
        d[nm] = nc.dram_tensor(nm, [128, 16], F32, kind="ExternalInput")
    d["bg1_c"] = nc.dram_tensor("bg1_c", [128, 4], F32, kind="ExternalInput")
    d["out"] = nc.dram_tensor("out", [D, HL], F32, kind="ExternalOutput")
    d["hs_scr"] = nc.dram_tensor("hs_scr", [16, 128, 4 * T], BF16, kind="Internal")

    dbg = {}
    if DEBUG:
        for tag, shp, dt in [("xn0", [128, NDC * TS], BF16), ("g0", [128, NRC * T], BF16),
                             ("go0", [128, NRC * T], BF16),
                             ("Y0", [128, NDC * T], F32), ("yn0", [128, NDC * T], BF16),
                             ("carry0", [128, 16], F32), ("s0col", [128, 16], F32),
                             ("xnp0", [128, NDC * T], BF16)]:
            dbg[tag] = nc.dram_tensor("dbg_" + tag, shp, dt, kind="ExternalOutput")

    def S(name, shape, dt=F32):
        return nc.alloc_sbuf_tensor(name, shape, dt)

    xn = [S("xn0", [128, NDC * TS], BF16), S("xn1", [128, NDC * TS], BF16)]
    xbf = S("xbf", [128, 3 * T], BF16)
    xf32 = S("xf32", [128, 3 * T])
    xr = S("xr", [128, 2 * T])
    sqs = S("sqs", [128, 3 * T], BF16)
    g_sl = S("g_sl", [128, NRC * T], BF16)
    vgs = S("vgs", [128, T], BF16)
    goyn = S("goyn", [128, NRC * T], BF16)
    hfm = S("hfm", [128, NH4 * T], BF16)
    ug = S("ug", [128, T])
    kc = S("kc", [128, T])
    carry = S("carry", [128, 16])
    s0col = S("s0col", [128, 16])
    red1 = S("red1", [128, 1])
    Y = S("Y", [128, NDC * T])
    tmp1 = S("tmp1", [128, T])
    tcv = S("tcv", [128, T])
    hp = S("hp", [128, 4 * T], BF16)
    apl = S("apl", [128, 4 * T], BF16)
    tre = S("tre", [128, 4 * T], BF16)
    ssc = S("ssc", [128, 3 * T], BF16)
    hin = S("hin", [128, 3 * T], BF16)
    ost = S("ost", [128, 2 * T])
    mrow = S("mrow", [1, T])
    qrow = S("qrow", [1, T])
    sdrow = S("sdrow", [1, T])
    rrow = S("rrow", [1, T])
    nmrow = S("nmrow", [1, T])
    onesb = S("onesb", [128, 1], BF16)
    onesr = S("onesr", [1, 128])
    zero5 = S("zero5", [128, T], BF16)
    xnh = S("xnh", [128, 80], BF16)
    xhal = S("xhal", [128, 80])
    hrow = S("hrow", [1, 64])
    hb = S("hb", [128, 10])
    epsc = S("epsc", [1, 1])
    cols = {}
    for nm in ["bu_c", "bv_c", "bg2_c", "cb_c", "w0_c", "w1_c", "w2_c", "mask_c"]:
        cols[nm] = S(nm + "_s", [128, 16])
    cols["bg1_c"] = S("bg1_c_s", [128, 4])

    wb_g2 = S("wb_g2", [128, 2 * NH4 * 128], BF16)
    wb_uuv = S("wb_uuv", [128, 3 * NDC * 128], BF16)
    wb_vgu = S("wb_vgu", [128, 6 * NHC * 128], BF16)
    wb_dn = S("wb_dn", [128, 4 * 8 * 128], BF16)

    P = [nc.alloc_psum_tensor(f"P{i}", [128, T], F32) for i in range(8)]

    def bankwait(j):
        return k.free(("bank", j))

    def bankread(j, mark):
        k.setfree(("bank", j), mark)

    castmark = {}

    def bulk_cast(mat, s0, nsl):
        m = k.dma("s_cast", "pool", lambda e, a=mat, b=s0, n_=nsl: e.dma_start(
            d["sA_" + a][b:b + n_, :, :], d["A_" + a][b:b + n_, :, :]))
        for sl_ in range(s0, s0 + nsl):
            castmark[(mat, sl_)] = m

    class WSBuf:
        def __init__(self, buf, slab_free, nslots):
            self.buf, self.w = buf, slab_free
            self.lastuse = [[] for _ in range(nslots)]

        def view(self, slot):
            return self.buf[:, slot * self.w:(slot + 1) * self.w]

        def load(self, mat, slab, slot):
            dst = self.view(slot)
            srcb = d["sA_" + mat][slab, :, :]
            waits = list(self.lastuse[slot]) + [castmark[(mat, slab)]]
            m = k.dma("s_w", "sp", lambda e, a=dst, b=srcb: e.dma_start(a, b),
                      waits=waits)
            self.lastuse[slot] = []
            return m

        def used(self, slot, mark):
            if mark is not None:
                self.lastuse[slot].append(mark)

    ws_g2 = WSBuf(wb_g2, NH4 * 128, 2)
    ws_uuv = WSBuf(wb_uuv, NDC * 128, 3)
    ws_g1 = ws_uuv
    ws_vgu = WSBuf(wb_vgu, NHC * 128, 6)
    ws_dn = WSBuf(wb_dn, 8 * 128, 4)

    def dump(tag, src_ap, waits, fence=()):
        if DEBUG and tag in DUMP and tag in dbg:
            mk = k.dma("s_dbg", "sp",
                       lambda e, s=src_ap, t=tag: e.dma_start(dbg[t][:, :], s),
                       waits=waits)
            for key in fence:
                k.addfree(key, mk)
            return mk
        return None

    def ln_rows_bcast(msum, msq):
        k.emit("dve", lambda e: e.tensor_scalar_mul(mrow[:, :], P[6][0:1, :], 1.0 / D),
               waits=[msum] + k.free("rows"), inc=("dve", 1))
        k.emit("dve", lambda e: e.tensor_scalar_mul(qrow[:, :], P[7][0:1, :], 1.0 / D),
               waits=[msq], inc=("dve", 1))
        k.emit("dve", lambda e: e.tensor_mul(sdrow[:, :], mrow[:, :], mrow[:, :]),
               inc=("dve", 1))
        m4 = k.emit("dve", lambda e: e.tensor_sub(qrow[:, :], qrow[:, :], sdrow[:, :]),
                    inc=("dve", 1))
        m5 = k.emit("act", lambda e: e.activation(sdrow[:, :], qrow[:, :], AF.Sqrt,
                                                  bias=epsc[0:1, 0:1]),
                    waits=[m4, m_eps], inc=("act", 1))
        k.emit("dve", lambda e: e.reciprocal(rrow[:, :], sdrow[:, :]),
               waits=[m5], inc=("dve", 1))
        m7 = k.emit("dve", lambda e: e.scalar_tensor_tensor(
            nmrow[:, :], mrow[:, :], -1.0, rrow[:, :], OP.mult, OP.mult),
            inc=("dve", 1))
        mb1 = k.emit("pe", lambda e: e.matmul(P[6][:, :], onesr[:, :], rrow[:, :],
                                              start=True, stop=True),
                     waits=[m7, m_onesr], inc=("pe", 1))
        mb2 = k.emit("pe", lambda e: e.matmul(P[7][:, :], onesr[:, :], nmrow[:, :],
                                              start=True, stop=True),
                     waits=[m7], inc=("pe", 1))
        k.setfree("rows", mb2)
        return mb1, mb2

    def ln_full(dst_xn, x_dram, col0, W, off, dst_war):
        msum = msq = None
        ldm = []
        for j in range(3):
            ldm.append(k.dma("s_x", "act", lambda e, dc_=j, s_=j: e.dma_start(
                xf32[:, s_ * T:(s_ + 1) * T],
                x_dram[dc_ * 128:(dc_ + 1) * 128, col0:col0 + T]),
                waits=k.free(("xf32", j))))
        for dc in range(NDC):
            sl = dc % 3
            mcb = k.emit("act", lambda e, s_=sl: e.copy(
                xbf[:, s_ * T:(s_ + 1) * T], xf32[:, s_ * T:(s_ + 1) * T]),
                waits=[ldm[dc]] + k.free(("xbf", sl)), inc=("act", 1))
            k.setfree(("xf32", sl), mcb)
            if dc + 3 < NDC:
                ldm.append(k.dma("s_x", "act", lambda e, dc_=dc + 3, s_=sl: e.dma_start(
                    xf32[:, s_ * T:(s_ + 1) * T],
                    x_dram[dc_ * 128:(dc_ + 1) * 128, col0:col0 + T])))
            msum = k.emit("pe", lambda e, dc_=dc, s_=sl: e.matmul(
                P[6][0:1, :], onesb[:, :], xbf[:, s_ * T:(s_ + 1) * T],
                start=(dc_ == 0), stop=(dc_ == NDC - 1)),
                waits=[mcb, m_ones] + (bankwait(6) if dc == 0 else []),
                inc=("pe", 1))
            ma = k.emit("dve", lambda e, s_=sl: e.tensor_mul(
                sqs[:, s_ * T:(s_ + 1) * T], xbf[:, s_ * T:(s_ + 1) * T],
                xbf[:, s_ * T:(s_ + 1) * T]),
                waits=[mcb] + k.free(("sqs", sl)), inc=("dve", 1))
            msq = k.emit("pe", lambda e, dc_=dc, s_=sl: e.matmul(
                P[7][0:1, :], onesb[:, :], sqs[:, s_ * T:(s_ + 1) * T],
                start=(dc_ == 0), stop=(dc_ == NDC - 1)),
                waits=[ma] + (bankwait(7) if dc == 0 else []),
                inc=("pe", 1))
            k.setfree(("sqs", sl), msq)
            k.setfree(("xbf", sl), msq)
        mb1, mb2 = ln_rows_bcast(msum, msq)
        segm = []
        for dc in range(NDC):
            sl = dc % 2
            mx = k.dma("s_x", "act", lambda e, dc_=dc, s_=sl: e.dma_start(
                xr[:, s_ * T:(s_ + 1) * T],
                x_dram[dc_ * 128:(dc_ + 1) * 128, col0:col0 + T]),
                waits=k.free(("xr", sl)))
            mt = k.emit("dve", lambda e, s_=sl: e.tensor_mul(
                tmp1[:, :], xr[:, s_ * T:(s_ + 1) * T], P[6][:, :]),
                waits=[mx, mb1] + k.free("tmp1"), inc=("dve", 1))
            k.setfree(("xr", sl), mt)
            mn = k.emit("dve", lambda e, dc_=dc: e.tensor_add(
                dst_xn[:, dc_ * W + off:dc_ * W + off + T], tmp1[:, :], P[7][:, :]),
                waits=[mb2] + (list(dst_war) if dc == 0 else []), inc=("dve", 1))
            k.setfree("tmp1", mn)
            segm.append(mn)
        bankread(6, segm[-1])
        bankread(7, segm[-1])
        return segm

    bulk_cast("g1", 0, NH4)
    for p0 in range(0, NRC, 8):
        bulk_cast("g2", p0, 8)
    for p0 in range(0, NRC, 4):
        bulk_cast("u", p0, 4)

    mc = {}
    for nm in cols:
        mc[nm] = k.dma("s_w", "sp",
                       lambda e, n_=nm: e.dma_start(cols[n_][:, :], d[n_][:, :]))
    m_halo = k.dma("s_w", "sp", lambda e: e.dma_start(xhal[:, :], d["xhalo"][:, :]))
    m_ones = k.emit("dve", lambda e: e.memset(onesb[:, :], 1.0), inc=("dve", 1))
    m_onesr = k.emit("dve", lambda e: e.memset(onesr[:, :], 1.0), inc=("dve", 1))
    m_zero = k.emit("dve", lambda e: e.memset(zero5[:, :], 0.0), inc=("dve", 1))
    m_eps = k.emit("dve", lambda e: e.memset(epsc[:, :], EPS), inc=("dve", 1))

    def gphase(xn_ap, W, off, segm):
        hm = []
        for hc in range(NH4):
            b = hc % 4
            mw = ws_g1.load("g1", hc, hc % 2)
            pm = None
            for dc in range(NDC):
                pm = k.emit("pe", lambda e, hc_=hc, dc_=dc, b_=b, sl=hc % 2,
                            x_=xn_ap: e.matmul(
                    P[b_][:, :], ws_g1.view(sl)[:, dc_ * 128:(dc_ + 1) * 128],
                    x_[:, dc_ * W + off:dc_ * W + off + T],
                    start=(dc_ == 0), stop=(dc_ == NDC - 1)),
                    waits=[mw, segm[dc]] + (bankwait(b) if dc == 0 else []),
                    inc=("pe", 1) if dc == NDC - 1 else None)
            ws_g1.used(hc % 2, pm)
            me = k.emit("act", lambda e, hc_=hc, b_=b: e.activation(
                hfm[:, hc_ * T:(hc_ + 1) * T], P[b_][:, :], AF.Relu,
                bias=cols["bg1_c"][:, hc_:hc_ + 1]),
                waits=[pm, mc["bg1_c"]] + k.free(("hfm", hc)), inc=("act", 1))
            k.setfree(("hfm", hc))
            bankread(b, me)
            hm.append(me)
        gm = []
        for rc in range(NRC):
            b = rc % 6
            mw = ws_g2.load("g2", rc, rc % 2)
            pm = None
            for hc in range(NH4):
                pm = k.emit("pe", lambda e, rc_=rc, hc_=hc, b_=b, sl=rc % 2: e.matmul(
                    P[b_][:, :], ws_g2.view(sl)[:, hc_ * 128:(hc_ + 1) * 128],
                    hfm[:, hc_ * T:(hc_ + 1) * T],
                    start=(hc_ == 0), stop=(hc_ == NH4 - 1)),
                    waits=[mw, hm[hc]] + (bankwait(b) if hc == 0 else []),
                    inc=("pe", 1) if hc == NH4 - 1 else None)
            ws_g2.used(rc % 2, pm)
            for hc in range(NH4):
                k.addfree(("hfm", hc), pm)
            me = k.emit("act", lambda e, rc_=rc, b_=b: e.activation(
                g_sl[:, rc_ * T:(rc_ + 1) * T], P[b_][:, :], AF.Sigmoid,
                bias=cols["bg2_c"][:, rc_:rc_ + 1]),
                waits=[pm, mc["bg2_c"]] + k.free(("g", rc)), inc=("act", 1))
            bankread(b, me)
            k.setfree(("g", rc))
            gm.append(me)
        return gm

    xn_war = {0: [], 1: []}
    m_s0z = k.emit("dve", lambda e: e.memset(s0col[:, :], 0.0), inc=("dve", 1))
    segp_next = ln_full(xn[0], d["xpf"], 0, T, 0, [])
    for ss in range(NS):
        par_p = ss % 2
        xnp = xn[par_p]
        segp = segp_next
        mkd_xnp = dump("xnp0", xnp[:, 0:NDC * T], [segp[-1]]) if ss == 0 else None
        gm = gphase(xnp, T, 0, segp)
        if ss + 1 < NS:
            segp_next = ln_full(xn[1 - par_p], d["xpf"], (ss + 1) * T, T, 0,
                                xn_war[1 - par_p])
        war_pe = []
        for rc in range(NRC):
            b = rc % 6
            slot = rc % 3
            mw = ws_uuv.load("u", rc, slot)
            pm = None
            for dc in range(NDC):
                pm = k.emit("pe", lambda e, rc_=rc, dc_=dc, sl=slot, b_=b,
                            x_=xnp: e.matmul(
                    P[b_][:, :], ws_uuv.view(sl)[:, dc_ * 128:(dc_ + 1) * 128],
                    x_[:, dc_ * T:(dc_ + 1) * T],
                    start=(dc_ == 0), stop=(dc_ == NDC - 1)),
                    waits=[mw, segp[dc], gm[rc]] + (bankwait(b) if dc == 0 else []),
                    inc=("pe", 1) if dc == NDC - 1 else None)
            ws_uuv.used(slot, pm)
            mu = k.emit("dve", lambda e, rc_=rc, b_=b: e.scalar_tensor_tensor(
                ug[:, :], P[b_][:, :], cols["bu_c"][:, rc_:rc_ + 1],
                g_sl[:, rc_ * T:(rc_ + 1) * T], OP.add, OP.mult),
                waits=[pm, mc["bu_c"], m_s0z], inc=("dve", 1))
            bankread(b, mu)
            k.addfree(("g", rc), mu)
            k.emit("dve", lambda e: e.reduce_sum(red1[:, :], ug[:, :], axis=AX.X),
                   inc=("dve", 1))
            k.emit("dve", lambda e, rc_=rc: e.tensor_add(
                s0col[:, rc_:rc_ + 1], s0col[:, rc_:rc_ + 1], red1[:, :]),
                inc=("dve", 1))
            war_pe.append(pm)
        xn_war[par_p] = war_pe + ([mkd_xnp] if mkd_xnp is not None else [])
        if ss == 0:
            for p0 in range(0, NRC, 4):
                bulk_cast("v", p0, 4)
        elif ss == 1:
            for p0 in range(0, NRC, 4):
                bulk_cast("uv", p0, 4)

    m_hbf = k.emit("act", lambda e: e.copy(xnh[:, :], xhal[:, :]), waits=[m_halo],
                   inc=("act", 1))
    mhs = k.emit("pe", lambda e: e.matmul(P[6][0:1, 0:80], onesb[:, :], xnh[:, :],
                                          start=True, stop=True),
                 waits=[m_hbf, m_ones], inc=("pe", 1))
    mhq0 = k.emit("act", lambda e: e.activation(sqs[:, 0:80], xnh[:, :], AF.Square),
                  waits=[m_hbf] + k.free(("sqs", 0)), inc=("act", 1))
    mhq = k.emit("pe", lambda e: e.matmul(P[7][0:1, 0:80], onesb[:, :], sqs[:, 0:80],
                                          start=True, stop=True),
                 waits=[mhq0], inc=("pe", 1))
    k.setfree(("sqs", 0), mhq)
    mh1 = k.emit("dve", lambda e: e.reduce_sum(
        hrow[0:1, 0:5], P[6][0:1, 0:80].rearrange("p (j c) -> p j c", j=5), axis=AX.X),
        waits=[mhs], inc=("dve", 1))
    mh2 = k.emit("dve", lambda e: e.reduce_sum(
        hrow[0:1, 5:10], P[7][0:1, 0:80].rearrange("p (j c) -> p j c", j=5), axis=AX.X),
        waits=[mhq], inc=("dve", 1))
    k.emit("dve", lambda e: e.tensor_scalar_mul(hrow[0:1, 10:20], hrow[0:1, 0:10], 1.0 / D),
           inc=("dve", 1))
    k.emit("dve", lambda e: e.tensor_mul(hrow[0:1, 20:25], hrow[0:1, 10:15],
                                         hrow[0:1, 10:15]), inc=("dve", 1))
    mh5 = k.emit("dve", lambda e: e.tensor_sub(hrow[0:1, 25:30], hrow[0:1, 15:20],
                                               hrow[0:1, 20:25]), inc=("dve", 1))
    mh6 = k.emit("act", lambda e: e.activation(hrow[0:1, 30:35], hrow[0:1, 25:30],
                                               AF.Sqrt, bias=epsc[0:1, 0:1]),
                 waits=[mh5, m_eps], inc=("act", 1))
    mh7 = k.emit("dve", lambda e: e.reciprocal(hrow[0:1, 35:40], hrow[0:1, 30:35]),
                 waits=[mh6], inc=("dve", 1))
    mh8 = k.emit("dve", lambda e: e.scalar_tensor_tensor(
        hrow[0:1, 40:45], hrow[0:1, 10:15], -1.0, hrow[0:1, 35:40], OP.mult, OP.mult),
        inc=("dve", 1))
    mh9 = k.emit("pe", lambda e: e.matmul(P[6][:, 0:10], onesr[:, :],
                                          hrow[0:1, 35:45], start=True, stop=True),
                 waits=[mh8, m_onesr], inc=("pe", 1))
    mh10 = k.emit("dve", lambda e: e.tensor_copy(hb[:, :], P[6][:, 0:10]),
                  waits=[mh9], inc=("dve", 1))
    bankread(6, mh10)
    bankread(7, mh2)
    m_xnh = None
    for j in range(5):
        m_xnh = k.emit("dve", lambda e, j_=j: e.tensor_scalar(
            xnh[:, j_ * 16:(j_ + 1) * 16], xhal[:, j_ * 16:(j_ + 1) * 16],
            hb[:, j_:j_ + 1], hb[:, 5 + j_:6 + j_], OP.mult, OP.add),
            waits=[mh10], inc=("dve", 1))

    seg0 = ln_full(xn[0], d["xf"], 1, TS, 1, xn_war[0])
    m_lh0 = k.emit("dve", lambda e: e.tensor_copy(
        xn[0][:, 0:NDC * TS:TS], xnh[:, 0:16]), waits=[m_xnh], inc=("dve", 1))
    m_rh0 = k.emit("dve", lambda e: e.tensor_copy(
        xn[0][:, TS - 1:NDC * TS:TS], xnh[:, 16:32]), inc=("dve", 1))
    xn_seg = {0: seg0}
    xn_halo = {0: m_rh0}


    dump("s0col", s0col[:, :], [k.emit("dve", lambda e: e.nop(), inc=("dve", 1))])
    m_carry = k.emit("dve", lambda e: e.tensor_mul(
        carry[:, :], s0col[:, :], cols["mask_c"][:, :]),
        waits=[mc["mask_c"]], inc=("dve", 1))
    dump("carry0", carry[:, :], [m_carry], fence=[("carryd",)])

    y_ready = [None] * NDC
    ln2_cover = None

    for i in range(NS):
        par = i % 2
        col0 = 1 + i * T
        xni = xn[par]
        segm = xn_seg[i]

        gm = gphase(xni, TS, 1, segm)
        if i == 0:
            dump("g0", g_sl[:, :], [gm[-1]], fence=[("g", r_) for r_ in range(NRC)])

        if i == 0:
            for p0 in range(0, NIC, 4):
                bulk_cast("gate", p0, 4)
                bulk_cast("up", p0, 4)

        gom = []
        for rc in range(NRC):
            bv_ = (2 * rc) % 6
            bu_ = (2 * rc + 1) % 6
            mwv = ws_vgu.load("v", rc, rc % 3)
            pmv = None
            for dc in range(NDC):
                pmv = k.emit("pe", lambda e, rc_=rc, dc_=dc, sl=rc % 3, b_=bv_,
                             x_=xni: e.matmul(
                    P[b_][:, :], ws_vgu.view(sl)[:, dc_ * 128:(dc_ + 1) * 128],
                    x_[:, dc_ * TS + 1:dc_ * TS + 1 + T],
                    start=(dc_ == 0), stop=(dc_ == NDC - 1)),
                    waits=[mwv, segm[dc], gm[rc]] + (bankwait(bv_) if dc == 0 else []),
                    inc=("pe", 1) if dc == NDC - 1 else None)
            ws_vgu.used(rc % 3, pmv)
            mv = k.emit("dve", lambda e, rc_=rc, b_=bv_: e.scalar_tensor_tensor(
                vgs[:, :], P[b_][:, :], cols["bv_c"][:, rc_:rc_ + 1],
                g_sl[:, rc_ * T:(rc_ + 1) * T], OP.add, OP.mult),
                waits=[pmv, mc["bv_c"]], inc=("dve", 1))
            bankread(bv_, mv)
            mwu = ws_uuv.load("u", rc, rc % 3)
            pmu = None
            for dc in range(NDC):
                pmu = k.emit("pe", lambda e, rc_=rc, dc_=dc, sl=rc % 3, b_=bu_,
                             x_=xni: e.matmul(
                    P[b_][:, :], ws_uuv.view(sl)[:, dc_ * 128:(dc_ + 1) * 128],
                    x_[:, dc_ * TS + 1:dc_ * TS + 1 + T],
                    start=(dc_ == 0), stop=(dc_ == NDC - 1)),
                    waits=[mwu, segm[dc]] + (bankwait(bu_) if dc == 0 else []),
                    inc=("pe", 1) if dc == NDC - 1 else None)
            ws_uuv.used(rc % 3, pmu)
            mu = k.emit("dve", lambda e, rc_=rc, b_=bu_: e.scalar_tensor_tensor(
                ug[:, :], P[b_][:, :], cols["bu_c"][:, rc_:rc_ + 1],
                g_sl[:, rc_ * T:(rc_ + 1) * T], OP.add, OP.mult),
                waits=[pmu], inc=("dve", 1))
            bankread(bu_, mu)
            k.addfree(("g", rc), mu)
            k.emit("dve", lambda e, rc_=rc: e.tensor_tensor_scan(
                kc[:, :], ug[:, :], zero5[:, :], carry[:, rc_:rc_ + 1],
                OP.add, OP.add),
                waits=[m_zero, m_carry], inc=("dve", 1))
            k.emit("dve", lambda e, rc_=rc: e.tensor_copy(
                carry[:, rc_:rc_ + 1], kc[:, T - 1:T]),
                waits=k.free(("carryd",)), inc=("dve", 1))
            mg = k.emit("dve", lambda e, rc_=rc: e.tensor_mul(
                goyn[:, rc_ * T:(rc_ + 1) * T], kc[:, :], vgs[:, :]),
                waits=k.free(("gy", rc)), inc=("dve", 1))
            k.setfree(("gy", rc))
            gom.append(mg)
            dc = rc
            base = dc * TS
            c1 = k.emit("pool", lambda e, b_=base, dc_=dc, x_=xni: e.tensor_scalar(
                tcv[:, :], x_[:, b_:b_ + T], cols["w0_c"][:, dc_:dc_ + 1],
                cols["cb_c"][:, dc_:dc_ + 1], OP.mult, OP.add),
                waits=[segm[dc], xn_halo[i], mc["w0_c"], mc["cb_c"]] + k.free("tcv"),
                inc=("pool", 1))
            conv_c1_last = c1
            k.emit("dve", lambda e, b_=base, dc_=dc, x_=xni: e.scalar_tensor_tensor(
                tcv[:, :], x_[:, b_ + 1:b_ + 1 + T], cols["w1_c"][:, dc_:dc_ + 1],
                tcv[:, :], OP.mult, OP.add),
                waits=[mc["w1_c"], c1], inc=("dve", 1))
            c3 = k.emit("dve", lambda e, b_=base, dc_=dc, x_=xni: e.scalar_tensor_tensor(
                tcv[:, :], x_[:, b_ + 2:b_ + 2 + T], cols["w2_c"][:, dc_:dc_ + 1],
                tcv[:, :], OP.mult, OP.add),
                waits=[mc["w2_c"]], inc=("dve", 1))
            mx = k.dma("s_x", "act", lambda e, dc_=dc, s_=dc % 2, c0=col0: e.dma_start(
                xr[:, s_ * T:(s_ + 1) * T],
                d["xf"][dc_ * 128:(dc_ + 1) * 128, c0:c0 + T]),
                waits=k.free(("xr", dc % 2)))
            cy = k.emit("dve", lambda e, dc_=dc, s_=dc % 2: e.tensor_add(
                Y[:, dc_ * T:(dc_ + 1) * T], tcv[:, :], xr[:, s_ * T:(s_ + 1) * T]),
                waits=[c3, mx] + k.free(("Y", dc)), inc=("dve", 1))
            k.setfree(("xr", dc % 2), cy)
            k.setfree("tcv", cy)
            k.setfree(("Y", dc))
            y_ready[dc] = cy
        if i == 0:
            dump("go0", goyn[:, :], [gom[-1]], fence=[("gy", r_) for r_ in range(NRC)])

        grp = None
        for half in range(2):
            rc_marks = []
            for rc in range(NRC):
                slot = rc % 3
                mw = ws_uuv.load("uv", rc, slot)
                pm = None
                for j in range(8):
                    dc = half * 8 + j
                    w0 = [mw, gom[rc]]
                    if rc == 0:
                        w0 = w0 + bankwait(j)
                    pm = k.emit("pe", lambda e, rc_=rc, dc_=dc, j_=j, sl=slot: e.matmul(
                        P[j_][:, :], ws_uuv.view(sl)[:, dc_ * 128:(dc_ + 1) * 128],
                        goyn[:, rc_ * T:(rc_ + 1) * T],
                        start=(rc_ == 0), stop=(rc_ == NRC - 1)),
                        waits=w0, inc=("pe", 1) if j == 7 else None)
                ws_uuv.used(slot, pm)
                rc_marks.append(pm)
            grp = rc_marks[-1]
            for rc in range(NRC):
                k.addfree(("gy", rc), grp)
            xn_war[par] = [grp, conv_c1_last]
            for j in range(8):
                dc = half * 8 + j
                my = k.emit("dve", lambda e, dc_=dc, j_=j: e.tensor_add(
                    Y[:, dc_ * T:(dc_ + 1) * T], P[j_][:, :],
                    Y[:, dc_ * T:(dc_ + 1) * T]),
                    waits=[grp, y_ready[dc]], inc=("dve", 1))
                bankread(j, my)
                y_ready[dc] = my

        if i == 0:
            for p0 in range(0, 2 * NIC, 8):
                bulk_cast("dn", p0, 8)
        msum = None
        for dc in range(NDC):
            sl = dc % 2
            w0 = [y_ready[dc]] + k.free(("y2", sl))
            if dc < 2:
                w0 += k.free(("xbf", 0)) + k.free(("xbf", 1))
            mcb = k.emit("act", lambda e, dc_=dc, s_=sl: e.copy(
                xbf[:, s_ * T:(s_ + 1) * T], Y[:, dc_ * T:(dc_ + 1) * T]),
                waits=w0, inc=("act", 1))
            msum = k.emit("pe", lambda e, dc_=dc, s_=sl: e.matmul(
                P[6][0:1, :], onesb[:, :], xbf[:, s_ * T:(s_ + 1) * T],
                start=(dc_ == 0), stop=(dc_ == NDC - 1)),
                waits=[mcb] + (bankwait(6) if dc == 0 else []), inc=("pe", 1))
            k.setfree(("y2", sl), msum)
        msq = None
        for dc in range(NDC):
            sl = dc % 2
            mcb2 = k.emit("act", lambda e, dc_=dc, s_=sl: e.copy(
                xbf[:, s_ * T:(s_ + 1) * T], Y[:, dc_ * T:(dc_ + 1) * T]),
                waits=k.free(("y2", sl)), inc=("act", 1))
            ma = k.emit("act", lambda e, s_=sl: e.activation(
                sqs[:, s_ * T:(s_ + 1) * T], xbf[:, s_ * T:(s_ + 1) * T], AF.Square),
                waits=k.free(("sqs", sl)), inc=("act", 1))
            msq = k.emit("pe", lambda e, dc_=dc, s_=sl: e.matmul(
                P[7][0:1, :], onesb[:, :], sqs[:, s_ * T:(s_ + 1) * T],
                start=(dc_ == 0), stop=(dc_ == NDC - 1)),
                waits=[ma] + (bankwait(7) if dc == 0 else []), inc=("pe", 1))
            k.setfree(("sqs", sl), msq)
            k.setfree(("y2", sl), msq)
        k.setfree(("xbf", 0), msq)
        k.setfree(("xbf", 1), msq)
        ln2_cover = msq
        mb1, mb2 = ln_rows_bcast(msum, msq)
        yn_m = []
        for dc in range(NDC):
            mt = k.emit("dve", lambda e, dc_=dc: e.tensor_mul(
                tmp1[:, :], Y[:, dc_ * T:(dc_ + 1) * T], P[6][:, :]),
                waits=[mb1] + k.free("tmp1"), inc=("dve", 1))
            mn = k.emit("dve", lambda e, dc_=dc: e.tensor_add(
                goyn[:, dc_ * T:(dc_ + 1) * T], tmp1[:, :], P[7][:, :]),
                waits=[mb2] + k.free(("gy", dc)), inc=("dve", 1))
            k.setfree("tmp1", mn)
            k.setfree(("gy", dc))
            yn_m.append(mn)
        bankread(6, yn_m[-1])
        bankread(7, yn_m[-1])
        if i == 0:
            dump("yn0", goyn[:, :], [yn_m[-1]], fence=[("gy", d_) for d_ in range(NDC)])
        dump("Y0", Y[:, :], [y_ready[NDC - 1]],
             fence=[("Y", d_) for d_ in range(NDC)]) if i == 0 else None

        hs_w = []
        for gc in range(NRC):
            abs_m = []
            for pl in range(4):
                ic = pl * 16 + gc
                sg = (gc * 4 + pl) % 3
                slg = (gc * 4 + pl) % 3
                mwg = ws_vgu.load("gate", ic, slg)
                mwu = ws_vgu.load("up", ic, 3 + slg)
                pg = None
                for hc in range(NHC):
                    pg = k.emit("pe", lambda e, hc_=hc, s_=sg, sl=slg: e.matmul(
                        P[s_][:, :], ws_vgu.view(sl)[:, hc_ * 128:(hc_ + 1) * 128],
                        goyn[:, hc_ * T:(hc_ + 1) * T],
                        start=(hc_ == 0), stop=(hc_ == NHC - 1)),
                        waits=[mwg, yn_m[hc]] + (bankwait(sg) if hc == 0 else []),
                        inc=("pe", 1) if hc == NHC - 1 else None)
                pu = None
                for hc in range(NHC):
                    pu = k.emit("pe", lambda e, hc_=hc, s_=sg, sl=3 + slg: e.matmul(
                        P[3 + s_][:, :], ws_vgu.view(sl)[:, hc_ * 128:(hc_ + 1) * 128],
                        goyn[:, hc_ * T:(hc_ + 1) * T],
                        start=(hc_ == 0), stop=(hc_ == NHC - 1)),
                        waits=[mwu, yn_m[hc]] + (bankwait(3 + sg) if hc == 0 else []),
                        inc=("pe", 1) if hc == NHC - 1 else None)
                ws_vgu.used(slg, pg)
                ws_vgu.used(3 + slg, pu)
                for dc in range(NDC):
                    k.addfree(("gy", dc), pu)
                msl = k.emit("act", lambda e, s_=sg: e.activation(
                    ssc[:, s_ * T:(s_ + 1) * T], P[s_][:, :], AF.Silu),
                    waits=[pg] + k.free(("ssc", sg)), inc=("act", 1))
                bankread(sg, msl)
                mh = k.emit("dve", lambda e, pl_=pl, s_=sg: e.tensor_mul(
                    hp[:, pl_ * T:(pl_ + 1) * T], P[3 + s_][:, :],
                    ssc[:, s_ * T:(s_ + 1) * T]),
                    waits=[pu, msl] + k.free(("hp", pl)), inc=("dve", 1))
                bankread(3 + sg, mh)
                k.setfree(("ssc", sg), mh)
                k.setfree(("hp", pl))
                ma = k.emit("act", lambda e, pl_=pl: e.activation(
                    apl[:, pl_ * T:(pl_ + 1) * T], hp[:, pl_ * T:(pl_ + 1) * T],
                    AF.Abs),
                    waits=[mh] + k.free(("apl", pl)), inc=("act", 1))
                k.setfree(("apl", pl))
                abs_m.append(ma)
            t1 = k.emit("dve", lambda e: e.tensor_tensor(
                tre[:, 0:T], apl[:, 0:T], apl[:, T:2 * T], OP.min),
                waits=abs_m[0:2] + k.free("tre"), inc=("dve", 1))
            k.emit("dve", lambda e: e.tensor_tensor(
                tre[:, T:2 * T], apl[:, 0:T], apl[:, T:2 * T], OP.max), inc=("dve", 1))
            k.emit("dve", lambda e: e.tensor_tensor(
                tre[:, 2 * T:3 * T], apl[:, 2 * T:3 * T], apl[:, 3 * T:4 * T], OP.min),
                waits=abs_m[2:4], inc=("dve", 1))
            k.emit("dve", lambda e: e.tensor_tensor(
                tre[:, 3 * T:4 * T], apl[:, 2 * T:3 * T], apl[:, 3 * T:4 * T], OP.max),
                inc=("dve", 1))
            k.emit("dve", lambda e: e.tensor_tensor(
                tre[:, 0:T], tre[:, 0:T], tre[:, 2 * T:3 * T], OP.max), inc=("dve", 1))
            k.emit("dve", lambda e: e.tensor_tensor(
                tre[:, T:2 * T], tre[:, T:2 * T], tre[:, 3 * T:4 * T], OP.min),
                inc=("dve", 1))
            t7 = k.emit("dve", lambda e: e.tensor_tensor(
                tre[:, 2 * T:3 * T], tre[:, 0:T], tre[:, T:2 * T], OP.min),
                inc=("dve", 1))
            mm = []
            for pl in range(4):
                k.emit("dve", lambda e, pl_=pl: e.tensor_tensor(
                    tre[:, 3 * T:4 * T], apl[:, pl_ * T:(pl_ + 1) * T],
                    tre[:, 2 * T:3 * T], OP.is_gt),
                    waits=[t7], inc=("dve", 1))
                mk2 = k.emit("dve", lambda e, pl_=pl: e.tensor_mul(
                    apl[:, pl_ * T:(pl_ + 1) * T], hp[:, pl_ * T:(pl_ + 1) * T],
                    tre[:, 3 * T:4 * T]),
                    inc=("dve", 1))
                k.addfree(("hp", pl), mk2)
                mm.append(mk2)
            k.addfree("tre", mm[-1])
            mwr = k.dma("s_hs", "pool", lambda e, gc_=gc: e.dma_start(
                d["hs_scr"][gc_, :, :], apl[:, :]),
                waits=[mm[-1]] + k.free(("hsd", gc)))
            k.setfree(("hsd", gc))
            for pl in range(4):
                k.addfree(("apl", pl), mwr)
            hs_w.append(mwr)
            if gc == 0 and i + 1 < NS:
                segn = ln_full(xn[1 - par], d["xf"], 1 + (i + 1) * T, TS, 1,
                               xn_war[1 - par])
                mhl = k.emit("dve", lambda e, p_=par: e.tensor_copy(
                    xn[1 - p_][:, 0:NDC * TS:TS], xn[p_][:, T:NDC * TS:TS]),
                    inc=("dve", 1))
                mhr = k.emit("dve", lambda e, p_=par, i_=i: e.tensor_copy(
                    xn[1 - p_][:, TS - 1:NDC * TS:TS],
                    xnh[:, (i_ + 2) * 16:(i_ + 3) * 16]),
                    waits=[m_xnh], inc=("dve", 1))
                xn_seg[i + 1] = segn
                xn_halo[i + 1] = mhr

        for half in range(2):
            first_w = [bankwait(j) for j in range(8)]
            pm = None
            for jj in range(NIC):
                gc, pl = jj // 4, jj % 4
                ic = pl * 16 + gc
                sl = jj % 3
                sw = jj % 4
                md = ws_dn.load("dn", half * NIC + ic, sw)
                mi = k.dma("s_hsr", "act", lambda e, gc_=gc, pl_=pl, s_=sl: e.dma_start(
                    hin[:, s_ * T:(s_ + 1) * T],
                    d["hs_scr"][gc_, :, pl_ * T:(pl_ + 1) * T]),
                    waits=[hs_w[gc]] + k.free(("hin", sl)))
                k.addfree(("hsd", gc), mi)
                for hc8 in range(8):
                    w0 = [md, mi]
                    if jj == 0:
                        w0 = w0 + first_w[hc8]
                    pm = k.emit("pe", lambda e, h8=hc8, s_=sl, s2=sw, jj_=jj: e.matmul(
                        P[h8][:, :], ws_dn.view(s2)[:, h8 * 128:(h8 + 1) * 128],
                        hin[:, s_ * T:(s_ + 1) * T],
                        start=(jj_ == 0), stop=(jj_ == NIC - 1)),
                        waits=w0, inc=("pe", 1) if hc8 == 7 else None)
                ws_dn.used(sw, pm)
                k.setfree(("hin", sl), pm)
            grp_dn = pm
            for hc8 in range(8):
                hc = half * 8 + hc8
                sl = hc % 2
                mo = k.emit("dve", lambda e, hc_=hc, h8=hc8, s_=sl: e.tensor_add(
                    ost[:, s_ * T:(s_ + 1) * T], P[h8][:, :],
                    Y[:, hc_ * T:(hc_ + 1) * T]),
                    waits=[grp_dn] + k.free(("ost", sl)) + k.free(("Y", hc)),
                    inc=("dve", 1))
                bankread(hc8, mo)
                k.setfree(("Y", hc), mo, ln2_cover)
                mo2 = k.dma("s_out", "sp", lambda e, hc_=hc, s_=sl, i_=i: e.dma_start(
                    d["out"][hc_ * 128:(hc_ + 1) * 128, i_ * T:(i_ + 1) * T],
                    ost[:, s_ * T:(s_ + 1) * T]),
                    waits=[mo])
                k.setfree(("ost", sl), mo2)

    k.emit("pool", lambda e: e.nop(),
           waits=(k.stream_tail("s_out") + k.stream_tail("s_dbg")
                  + k.stream_tail("s_cast") + k.stream_tail("s_hs")),
           inc=("pool", 1))

    k.finalize()
    return nc


LAST_RESULTS = None


def kernel(**inputs):
    global LAST_RESULTS
    in_maps = stage_inputs(inputs)
    nc = build()
    res = run_bass_kernel_spmd(nc, in_maps, core_ids=list(range(8)))
    LAST_RESULTS = res
    out = np.empty((B, L, D), dtype=np.float32)
    for c in range(8):
        b, h = c // 2, c % 2
        out[b, h * HL:(h + 1) * HL, :] = res.results[c]["out"].T
    return out


if __name__ == "__main__":
    import reference as ref
    inp = ref.setup_inputs(0)
    got = kernel(**{k_: np.asarray(v) for k_, v in inp.items()})
    want = np.asarray(ref.reference(**inp))
    print("var ratio:", float(np.mean((got - want) ** 2) / np.mean(want ** 2)))
    print("max abs:", float(np.abs(got - want).max()))
